# version 07
# speedup vs baseline: 1.3809x; 1.1558x over previous
import numpy as np
import ml_dtypes
import concourse.bass as bass
import concourse.mybir as mybir
from concourse.bass_utils import run_bass_kernel_spmd
from contextlib import ExitStack

F32 = mybir.dt.float32
F16 = mybir.dt.float16
FP8 = mybir.dt.float8e4
AF = mybir.ActivationFunctionType
OP = mybir.AluOpType
AX = mybir.AxisListType.X

NCORES = 8
T = 2048
CH = 256
NCH = T // CH
D = 1024
DSSM = 2048
NH = 32
HD = 64
DPROJ = 4384
HF = 4096
KCONV = 4
NXT = 18
CVW = CH + 3
LN_EPS = 1e-5
RMS_EPS = 1e-12

ENGS = ("sync", "gpsimd", "vector", "scalar", "tensor")

DEBUG = {"stop": None}


NDS = 12
NDG = 4


class Bld:
    def __init__(self, nc, ctx):
        self.nc = nc
        self.q = {e: [] for e in ENGS}
        self.sem = {}
        keys = ["v", "s", "t", "g"] + ["d%d" % i for i in range(NDS)] + ["e%d" % i for i in range(NDG)]
        for k in keys:
            self.sem[k] = ctx.enter_context(nc.semaphore("sem_" + k))
        self.cnt = {k: 0 for k in self.sem}
        self.waited = {e: {} for e in ENGS}
        self.dsrot = 0
        self.dgrot = 0

    def prod(self, eng, key, f, n=1):
        self.cnt[key] += n
        sem = self.sem[key]
        def g(e):
            f(e).then_inc(sem, n)
        self.q[eng].append(g)
        return (key, self.cnt[key])

    def dma(self, eng, f, n=1):
        if eng == "gpsimd":
            key = "e%d" % (self.dgrot % NDG)
            self.dgrot += 1
        else:
            key = "d%d" % (self.dsrot % NDS)
            self.dsrot += 1
        self.cnt[key] += 16 * n
        sem = self.sem[key]
        def g(e):
            r = f(e)
            lst = list(r) if isinstance(r, (list, tuple)) else [r]
            assert len(lst) == n, (len(lst), n)
            for i in lst:
                i.then_inc(sem, 16)
        self.q[eng].append(g)
        return (key, self.cnt[key])

    def w(self, eng, *toks):
        for tk in toks:
            if tk is None:
                continue
            key, val = tk
            if val <= self.waited[eng].get(key, 0):
                continue
            self.waited[eng][key] = val
            sem = self.sem[key]
            self.q[eng].append(lambda e, sem=sem, val=val: e.wait_ge(sem, val))

    def now(self, key):
        return (key, self.cnt[key])

    def barrier(self):
        snap = [(k, v) for k, v in self.cnt.items()]
        for eng in ENGS:
            self.w(eng, *snap)

    def finalize(self, block):
        for eng in ENGS:
            fs = self.q[eng]
            def runner(e, fs=fs):
                for f in fs:
                    f(e)
            getattr(block, eng)(runner)


def build_nc():
    nc = bass.Bass("TRN2", target_bir_lowering=False, debug=False)
    inp = {}

    def I(name, shape, dt=F32):
        inp[name] = nc.dram_tensor(name, list(shape), dt, kind="ExternalInput").ap()
        return inp[name]

    u = I("u", [T, D])
    u_halo = I("u_halo", [128, D])
    wint_h = I("wint", [128, 8 * DPROJ])
    w1t_h = I("w1t", [128, 8 * HF])
    w2t_h = I("w2t", [128, 32 * 1024])
    woutt_h = I("woutt", [128, 16 * D])
    b1_col = I("b1_col", [128, HF // 128])
    bin_col = I("bin_col", [128, NXT + 1])
    bz_row = I("bz_row", [1, DSSM], F16)
    b2_row = I("b2_row", [1, D])
    bout_row = I("bout_row", [1, D], F16)
    convw_col = I("convw_col", [128, NXT * KCONV])
    convb_col = I("convb_col", [128, NXT])
    dtbias_col = I("dtbias_col", [32, 1])
    alog_col = I("alog_col", [32, 1])
    d_row = I("d_row", [1, NH])
    normw_col = I("normw_col", [128, DSSM // 128])
    ident16 = I("ident16", [128, 128], F16)
    ident32 = I("ident32", [32, 32])
    ones_row = I("ones_row", [1, 128])
    ones16 = I("ones16", [16, 128], F16)
    trilT = I("trilT", [128, 2 * CH], F16)
    eexp = I("eexp", [128, DSSM], F16)
    bdones = I("bdones", [128, 4 * CH], F16)
    segones = I("segones", [128, 128], F16)
    repl16 = I("repl16", [128, 128], F16)
    repl32 = I("repl32", [128, 128])
    mcols = I("mcols", [128, NCORES])
    imcols = I("imcols", [128, NCORES])
    mnext = I("mnext", [128, 1])
    imnext = I("imnext", [128, 1])
    halo_mask = I("halo_mask", [128, 1])

    out = nc.dram_tensor("out", [T, D], F32, kind="ExternalOutput").ap()

    x1_scr = nc.dram_tensor("x1_scr", [T, D], F32, kind="Internal").ap()
    x2_scr = nc.dram_tensor("x2_scr", [T, D], F32, kind="Internal").ap()
    xt_scr = nc.dram_tensor("xt_scr", [NCH, 128, 2 * DSSM], F16, kind="Internal").ap()
    sz_scr = nc.dram_tensor("sz_scr", [NCH, 128, 2 * DSSM], F16, kind="Internal").ap()
    sl_scr = nc.dram_tensor("sl_scr", [NCH - 1, 128, DSSM], F32, kind="Internal").ap()
    bc_scr = nc.dram_tensor("bc_scr", [NCH, 128, 2 * CH], F16, kind="Internal").ap()
    bd_scr = nc.dram_tensor("bd_scr", [64, 2 * CH], F16, kind="Internal").ap()
    w1t_scr = nc.dram_tensor("w1t_scr", [128, 8 * HF], F16, kind="Internal").ap()
    w2q_scr = nc.dram_tensor("w2q_scr", [128, 32 * 1024], FP8, kind="Internal").ap()
    winT_scr = nc.dram_tensor("winT_scr", [128, 8 * DPROJ], F16, kind="Internal").ap()
    woutT_scr = nc.dram_tensor("woutT_scr", [128, 16 * D], F16, kind="Internal").ap()
    cc_in = nc.dram_tensor("cc_in", [129, DSSM], F32, kind="Internal").ap()
    cc_out = nc.dram_tensor("cc_out", [NCORES * 129, DSSM], F32, kind="Internal").ap()

    ctx = ExitStack()
    with ctx:
        E = ctx.enter_context

        ar1 = E(nc.sbuf_tensor("ar1", [128, 8 * DPROJ], F16))
        arq = E(nc.sbuf_tensor("arq", [128, 32 * 1024], FP8))
        aq16 = arq[:].bitcast(F16)
        sm32 = E(nc.sbuf_tensor("sm32", [32, NCH * CH], F32))
        dt16a = E(nc.sbuf_tensor("dt16a", [128, NCH * CH], F16))
        bfc = E(nc.sbuf_tensor("bfc", [128, 2 * CH], F16))
        sm16 = E(nc.sbuf_tensor("sm16", [128, 2048], F16))
        smx = E(nc.sbuf_tensor("smx", [32, 32], F32))
        prun = E(nc.sbuf_tensor("prun", [1, DSSM], F32))
        c_id16 = E(nc.sbuf_tensor("c_id16", [128, 128], F16))
        c_id32 = E(nc.sbuf_tensor("c_id32", [32, 32], F32))
        c_ones = E(nc.sbuf_tensor("c_ones", [1, 128], F32))
        c_ones16 = E(nc.sbuf_tensor("c_ones16", [16, 128], F16))
        c_tril = E(nc.sbuf_tensor("c_tril", [128, 2 * CH], F16))
        c_eexp = E(nc.sbuf_tensor("c_eexp", [128, DSSM], F16))
        c_bdones = E(nc.sbuf_tensor("c_bdones", [128, 4 * CH], F16))
        c_segones = E(nc.sbuf_tensor("c_segones", [128, 128], F16))
        c_repl16 = E(nc.sbuf_tensor("c_repl16", [128, 128], F16))
        c_repl32 = E(nc.sbuf_tensor("c_repl32", [128, 128], F32))
        c_mcols = E(nc.sbuf_tensor("c_mcols", [128, NCORES], F32))
        c_imcols = E(nc.sbuf_tensor("c_imcols", [128, NCORES], F32))
        c_mnext = E(nc.sbuf_tensor("c_mnext", [128, 1], F32))
        c_imnext = E(nc.sbuf_tensor("c_imnext", [128, 1], F32))
        c_hmask = E(nc.sbuf_tensor("c_hmask", [128, 1], F32))
        c_b1 = E(nc.sbuf_tensor("c_b1", [128, HF // 128], F32))
        c_bin = E(nc.sbuf_tensor("c_bin", [128, NXT + 1], F32))
        c_convw = E(nc.sbuf_tensor("c_convw", [128, NXT * KCONV], F32))
        c_convb = E(nc.sbuf_tensor("c_convb", [128, NXT], F32))
        c_dtb = E(nc.sbuf_tensor("c_dtb", [32, 1], F32))
        c_alog = E(nc.sbuf_tensor("c_alog", [32, 1], F32))
        c_acol = E(nc.sbuf_tensor("c_acol", [32, 1], F32))
        c_dtcomb = E(nc.sbuf_tensor("c_dtcomb", [32, 1], F32))
        c_drow = E(nc.sbuf_tensor("c_drow", [1, NH], F32))
        c_normw = E(nc.sbuf_tensor("c_normw", [128, DSSM // 128], F32))

        PF32 = E(nc.sbuf_tensor("PF32", [128, 9744], F32))
        PF16 = E(nc.sbuf_tensor("PF16", [128, 12288], F16))
        PQ8 = E(nc.sbuf_tensor("PQ8", [128, 8192], FP8))
        x1h = PF16[:, 10240:11264]
        xtile2 = PF32[:, 5648:9744]
        c_b2q = sm16[0:1, 0:1024]
        rowmm = PF16[0:1, 10240:12288]
        c_b2rep = PF32[:, 5136:5648].bitcast(F16)

        wstg = PF32[:, 0:4096]
        wq16 = PF16[:, 0:4096]
        tstg = PF16[:, 4096:6144]
        wq8 = PQ8[:, 0:4096]
        xtile = PF32[:, 0:4096]
        lnt = PF32[:, 4096:5120]
        lnc_ = PF32[:, 5120:5136]
        xn = PF16[:, 0:4096]
        xnf = PF16[:, 4096:8192]
        o16 = PF16[:, 8192:12288]
        lnsq16 = o16[:, 0:1024] if False else PF16[:, 8192:9216]
        xnh = PF16[:, 9216:10240]
        hq = PQ8[:, 0:8192]
        cacc = PF32[:, 0:512]
        x1c = PF32[:, 512:2560]
        lnt2 = PF32[:, 2560:3584]
        lnc2_ = PF32[:, 3584:3600]
        smtp = PF32[:, 3600:4112]
        wstf = PF32[:, 4112:4368]
        repcd = PF32[:, 4368:4400]
        sloc = PF32[:, 4400:6448]
        gath = PF32[:, 6448:8496]
        dmcur_row = PF32[0:1, 0:2048]
        x1c2 = PF32[:, 6448:8496]
        xbc16 = PF16[:, 0:NXT * CVW]
        xn2 = PF16[:, 4704:6752]
        xnf2 = PF16[:, 6752:8800]
        xsil = PF16[:, 8800:9312]
        btt = PF16[:, 9312:9568]
        lnsq16b = PF16[:, 8800:9824]
        x1c3 = PF32[:, 0:2048]
        sused = PF32[:, 2048:4096]
        carry = PF32[:, 4096:6144]
        slpf = PF32[:, 6144:8192]
        tmp3 = PF32[:, 8192:8704]
        att = PF32[:, 8704:9728]
        rsc3 = PF32[:, 9728:9744]
        yt = PF16[:, 0:4096]
        yf = PF16[:, 4096:6144]
        sused16 = PF16[:, 6144:8192]
        boutrep = PF16[:, 8192:9216]
        xw = aq16[:, 0:4096]
        szt = aq16[:, 4096:8192]
        xtt = aq16[:, 8192:12288]
        szex = aq16[:, 12288:16384]
        AR1U = 16 * D
        bdseg = ar1[0:8, AR1U:AR1U + 8192]
        bdlhs = ar1[0:8, AR1U + 8192:AR1U + 10240]
        mtt2 = [ar1[:, AR1U + 10240:AR1U + 12288], ar1[:, AR1U + 12288:AR1U + 14336]]
        di16 = ar1[:, AR1U + 14336:AR1U + 18432]
        gmm = PF16[:, 9216:9728]
        bdsegF = ar1[:, AR1U:AR1U + 8192]
        bdlhsF = ar1[:, AR1U + 8192:AR1U + 10240]
        rowmmF = PF16[:, 10240:12288]
        smtpF = PF32[:, 4080:4112]
        dmcurF = PF32[:, 0:2048]
        dtf16 = sm16[:, 0:CH]
        wstf16 = sm16[:, 256:512]
        ecsf16 = sm16[:, 512:768]
        cshl = sm16[:, 768:1280]
        ncshl = sm16[:, 1280:1792]
        cdcol = smx[:, 0:1]
        sumcs = smx[:, 1:2]
        pruncol = smx[:, 2:3]
        lnco = smx[:, 3:8]

        pseg = E(nc.psum_tensor("pseg", [128, 1024], F32))
        pmm0 = E(nc.psum_tensor("pmm0", [128, 512], F32))
        pmm1 = E(nc.psum_tensor("pmm1", [128, 512], F32))
        paux = E(nc.psum_tensor("paux", [128, 1024], F32))
        pt0 = E(nc.psum_tensor("pt0", [128, 1024], F16))
        pt1 = E(nc.psum_tensor("pt1", [128, 1024], F16))
        PT = [pt0, pt1]
        PMM = [pmm0, pmm1]

        b = Bld(nc, ctx)

        DT = {}

        def p0():
            def loads(e):
                lst = []
                for src, dst in [
                    (ident16, c_id16), (ident32, c_id32), (ones_row, c_ones),
                    (ones16, c_ones16), (trilT, c_tril), (eexp, c_eexp),
                    (bdones, c_bdones), (segones, c_segones),
                    (repl16, c_repl16), (repl32, c_repl32),
                    (mcols, c_mcols), (imcols, c_imcols),
                    (mnext, c_mnext), (imnext, c_imnext),
                    (halo_mask, c_hmask), (b1_col, c_b1), (bin_col, c_bin),
                    (convw_col, c_convw), (convb_col, c_convb),
                    (dtbias_col, c_dtb), (alog_col, c_alog), (d_row, c_drow),
                    (normw_col, c_normw),
                ]:
                    lst.append(e.dma_start(out=dst[:], in_=src))
                lst.append(e.dma_start(out=wstg[0:1, 0:D], in_=b2_row))
                return lst
            ld0 = b.dma("sync", loads, 24)
            mz0 = b.prod("vector", "v", lambda e: (
                e.memset(sm16[:], 0.0), e.memset(dt16a[:], 0.0))[-1], 2)
            b.w("scalar", ld0)
            sA0 = b.prod("scalar", "s", lambda e: e.activation(out=c_acol[:], in_=c_alog[:], func=AF.Exp))
            b.w("scalar", sA0)
            b.prod("scalar", "s", lambda e: e.activation(out=c_acol[:], in_=c_acol[:], func=AF.Copy, scale=-1.0))
            b.w("vector", ld0, mz0)
            vq = b.prod("vector", "v", lambda e: (
                e.tensor_copy(out=c_b2q[:], in_=wstg[0:1, 0:D]),
                e.tensor_tensor(out=c_dtcomb[:], in0=c_bin[0:32, NXT:NXT + 1], in1=c_dtb[:], op=OP.add))[-1], 2)
            b.w("vector", vq)
            vq2 = b.prod("vector", "v", lambda e: e.tensor_copy(out=wstg[0:1, 0:D], in_=c_b2q[:]))
            b.w("tensor", vq2)
            tD = b.prod("tensor", "t", lambda e: (
                e.matmul(paux[:, 0:512], c_ones[:], wstg[0:1, 0:512], start=True, stop=True),
                e.matmul(paux[:, 512:1024], c_ones[:], wstg[0:1, 512:1024], start=True, stop=True),
                e.matmul(pseg[:, 0:NH], c_ones[:], c_drow[:], start=True, stop=True))[-1], 3)
            b.w("vector", tD)
            vD = b.prod("vector", "v", lambda e: (
                e.tensor_copy(out=c_b2rep[:, 0:512], in_=paux[:, 0:512]),
                e.tensor_copy(out=c_b2rep[:, 512:1024], in_=paux[:, 512:1024]),
                e.tensor_copy(out=repcd[:, 0:NH], in_=pseg[:, 0:NH]))[-1], 3)
            b.w("scalar", vD)
            def dis(e):
                for h in range(NH):
                    ins = e.activation(out=di16[:, h * 128:(h + 1) * 128], in_=c_id16[:],
                                       func=AF.Copy, scale=repcd[:, h:h + 1])
                return ins
            b.prod("scalar", "s", dis)
            b.barrier()

            for q in range(4):
                b.dma("gpsimd", lambda e, q=q: e.dma_start(
                    out=ar1[:, q * 8192:(q + 1) * 8192],
                    in_=w1t_h[:, q * 8192:(q + 1) * 8192]))
            for q in range(4):
                b.dma("gpsimd", lambda e, q=q: e.dma_start(
                    out=arq[:, q * 8192:(q + 1) * 8192],
                    in_=w2t_h[:, q * 8192:(q + 1) * 8192]))

            ftok = [None, None]
            stok = [None, None]
            for r in range(8):
                sl = wstg[:, (r % 2) * 2048:(r % 2) * 2048 + 2048]
                b.w("sync", stok[r % 2])
                dwi = b.dma("sync", lambda e, r=r, sl=sl: e.dma_start(
                    out=sl, in_=woutt_h[:, r * 2048:(r + 1) * 2048]))
                b.w("vector", dwi, ftok[r % 2])
                ftok[r % 2] = b.prod("vector", "v", lambda e, r=r, sl=sl: [
                    e.tensor_scalar(out=sl[:, i * 1024:(i + 1) * 1024],
                                    in0=sl[:, i * 1024:(i + 1) * 1024],
                                    scalar1=c_normw[:, r * 2 + i:r * 2 + i + 1],
                                    scalar2=None, op0=OP.mult) for i in range(2)][-1], 2)
                b.w("gpsimd", ftok[r % 2])
                stok[r % 2] = b.dma("gpsimd", lambda e, r=r, sl=sl: e.dma_start(
                    out=woutT_scr[:, r * 2048:(r + 1) * 2048], in_=sl))
            b.barrier()
            DT["w1scr"] = b.dma("gpsimd", lambda e: e.dma_start(out=w1t_scr, in_=w1t_h[:]))
            DT["w2scr"] = b.dma("gpsimd", lambda e: e.dma_start(out=w2q_scr, in_=w2t_h[:]))
            DT["winscr"] = b.dma("gpsimd", lambda e: e.dma_start(out=winT_scr, in_=wint_h[:]))

        def layer_norm(src_ap, dst_ap, tb, sqd, cols):
            b.w("scalar", b.now("s"))
            b.w("vector", b.now("v"))
            v0 = b.prod("vector", "v", lambda e: e.tensor_reduce(
                out=cols[:, 0:1], in_=src_ap, axis=AX, op=OP.add, negate=True))
            b.w("vector", v0)
            v0b = b.prod("vector", "v", lambda e: e.tensor_scalar(
                out=cols[:, 0:1], in0=cols[:, 0:1], scalar1=1.0 / D, scalar2=None, op0=OP.mult))
            b.w("scalar", v0b)
            sm = b.prod("scalar", "s", lambda e: e.activation(
                out=tb, in_=src_ap, func=AF.Identity, bias=cols[:, 0:1]))
            b.w("scalar", sm)
            s0 = b.prod("scalar", "s", lambda e: e.activation(
                out=sqd, in_=tb, func=AF.Square, accum_out=cols[:, 1:2]))
            b.w("vector", s0)
            v1 = b.prod("vector", "v", lambda e: e.tensor_scalar(
                out=cols[:, 1:2], in0=cols[:, 1:2], scalar1=1.0 / D, scalar2=LN_EPS,
                op0=OP.mult, op1=OP.add))
            b.w("scalar", v1)
            s1 = b.prod("scalar", "s", lambda e: e.activation(
                out=cols[:, 2:3], in_=cols[:, 1:2], func=AF.Ln))
            b.w("scalar", s1)
            s1b = b.prod("scalar", "s", lambda e: e.activation(
                out=cols[:, 3:4], in_=cols[:, 2:3], func=AF.Exp, scale=-0.5))
            b.w("scalar", s1b)
            s2 = b.prod("scalar", "s", lambda e: e.activation(
                out=dst_ap, in_=tb, func=AF.Copy, scale=cols[:, 3:4]))
            return s2

        def ffn_phase(src_dram, res_dram, with_halo):
            ntile = T // 512
            XT = [xtile, xtile2]
            slot_tok = [None, None]
            evtok = [None, None]
            sev = [None, None]
            vev = [None, None]
            hq_rtok = None
            for it in range(ntile + (1 if with_halo else 0)):
                halo = with_halo and it == ntile
                nsub = 1 if halo else 4
                xt = XT[it % 2]
                if slot_tok[it % 2] is not None:
                    b.w("sync", *slot_tok[it % 2])
                def ldf(e, it=it, halo=halo, nsub=nsub, xt=xt):
                    return [e.dma_start(out=xt[:, st_ * D:(st_ + 1) * D],
                                        in_=(u_halo if halo else
                                             src_dram[it * 512 + st_ * 128: it * 512 + (st_ + 1) * 128, :]))
                            for st_ in range(nsub)]
                ldc = b.dma("sync", ldf, nsub)
                b.w("vector", ldc)
                b.w("scalar", ldc)
                for st_ in range(nsub):
                    layer_norm(xt[:, st_ * D:(st_ + 1) * D], xn[:, st_ * D:(st_ + 1) * D],
                               lnt, lnsq16, lnc_)
                lnS = b.now("s")
                b.w("tensor", lnS)
                for kt in range(8):
                    pt = PT[kt % 2]
                    b.w("tensor", evtok[kt % 2])
                    def tr(e, kt=kt, nsub=nsub, pt=pt):
                        for st_ in range(nsub):
                            ins = e.transpose(pt[:, st_ * 128:(st_ + 1) * 128],
                                              xn[:, st_ * D + kt * 128: st_ * D + (kt + 1) * 128],
                                              c_id16[:])
                        return ins
                    tt = b.prod("tensor", "t", tr)
                    b.w("vector", tt)
                    ncols = nsub * 128
                    evtok[kt % 2] = b.prod("vector", "v",
                        lambda e, kt=kt, pt=pt, ncols=ncols: e.tensor_copy(
                            out=xnf[:, kt * 512: kt * 512 + ncols], in_=pt[:, 0:ncols]))
                vtr = b.now("v")
                b.w("tensor", vtr)
                nhalf = 1 if halo else 2
                hcols = 128 if halo else 256
                for hv in range(nhalf):
                    b.w("scalar", hq_rtok)
                    for m in range(32):
                        pb = PMM[m % 2]
                        b.w("tensor", sev[m % 2], vev[m % 2])
                        def mm(e, m=m, pb=pb, hv=hv, hcols=hcols):
                            for kt in range(8):
                                ins = e.matmul(pb[:, 0:hcols],
                                               ar1[:, kt * HF + m * 128: kt * HF + (m + 1) * 128],
                                               xnf[:, kt * 512 + hv * 256: kt * 512 + hv * 256 + hcols],
                                               start=(kt == 0), stop=(kt == 7))
                            return ins
                        tmc = b.prod("tensor", "t", mm)
                        b.w("scalar", tmc)
                        sev[m % 2] = b.prod("scalar", "s",
                            lambda e, m=m, pb=pb, hcols=hcols: e.activation(
                                out=hq[:, m * 256: m * 256 + hcols], in_=pb[:, 0:hcols],
                                func=AF.Relu, bias=c_b1[:, m:m + 1]))
                    hqS = b.now("s")
                    b.w("tensor", hqS)
                    for mh in range(1 if halo else 2):
                        mt = hv * 2 + mh
                        for nd in range(2):
                            qq = (2 * mh + nd) % 2
                            pb = PMM[qq]
                            b.w("tensor", vev[qq], sev[qq])
                            def mm(e, mh=mh, nd=nd, pb=pb):
                                hqr = hq[:].rearrange("p (j c) -> p j c", c=256)
                                aqr = arq[:].rearrange("p (j c) -> p j c", c=1024)
                                for k in range(16):
                                    ins = e.matmul(pb[:, 0:512],
                                                   hqr[:, 2 * k:2 * k + 2, mh * 128:(mh + 1) * 128],
                                                   aqr[:, 2 * k:2 * k + 2, nd * 512:(nd + 1) * 512],
                                                   start=(k == 0), stop=(k == 15),
                                                   perf_mode=mybir.MatmulPerfMode.DoubleRow)
                                return ins
                            tmc = b.prod("tensor", "t", mm)
                            b.w("vector", tmc)
                            vev[qq] = b.prod("vector", "v",
                                lambda e, mt=mt, nd=nd, pb=pb: e.tensor_tensor(
                                    out=o16[:, mt * D + nd * 512: mt * D + (nd + 1) * 512],
                                    in0=pb[:, 0:512], in1=c_b2rep[:, nd * 512:(nd + 1) * 512],
                                    op=OP.add))
                    hq_rtok = b.now("t")
                b.w("vector", b.now("v"))
                def res(e, nsub=nsub, halo=halo, xt=xt):
                    for mt in range(nsub):
                        ins = e.scalar_tensor_tensor(
                            out=(x1h[:, 0:D] if halo else xt[:, mt * D:(mt + 1) * D]),
                            in0=o16[:, mt * D:(mt + 1) * D], scalar=0.5,
                            in1=xt[:, mt * D:(mt + 1) * D], op0=OP.mult, op1=OP.add)
                    return ins
                rV = b.prod("vector", "v", res)
                if halo:
                    b.w("scalar", rV)
                    b.w("vector", rV)
                    layer_norm(x1h[:, 0:D], xnh[:], lnt, lnsq16, lnc_)
                else:
                    b.w("gpsimd", rV)
                    def stt(e, it=it, xt=xt):
                        return [e.dma_start(out=res_dram[it * 512 + mt * 128:
                                                         it * 512 + (mt + 1) * 128, :],
                                            in_=xt[:, mt * D:(mt + 1) * D]) for mt in range(4)]
                    sdc = b.dma("gpsimd", stt, 4)
                    slot_tok[it % 2] = (lnS, rV, sdc)
            b.barrier()

        def p2():
            b.w("sync", DT["winscr"])
            b.dma("sync", lambda e: e.dma_start(out=ar1[:, 0:8 * DPROJ], in_=winT_scr))
            b.dma("sync", lambda e: e.dma_start(out=rowmm, in_=bz_row))
            b.barrier()
            evtok = [None, None]
            for kt in range(8):
                pt = PT[kt % 2]
                b.w("tensor", evtok[kt % 2])
                tt = b.prod("tensor", "t", lambda e, kt=kt, pt=pt: e.transpose(
                    pt[:, 0:128], xnh[:, kt * 128:(kt + 1) * 128], c_id16[:]))
                b.w("vector", tt)
                evtok[kt % 2] = b.prod("vector", "v", lambda e, kt=kt, pt=pt: e.tensor_copy(
                    out=xnf2[:, kt * 256: kt * 256 + 128], in_=pt[:, 0:128]))
            b.w("tensor", b.now("v"))
            sev = [None, None]
            for mf in range(NXT):
                pb = PMM[mf % 2]
                b.w("tensor", sev[mf % 2])
                def mm(e, mf=mf, pb=pb):
                    for kt in range(8):
                        ins = e.matmul(pb[:, 0:128],
                                       ar1[:, kt * DPROJ + DSSM + mf * 128:
                                           kt * DPROJ + DSSM + (mf + 1) * 128],
                                       xnf2[:, kt * 256: kt * 256 + 128],
                                       start=(kt == 0), stop=(kt == 7))
                    return ins
                tmc = b.prod("tensor", "t", mm)
                b.w("scalar", tmc)
                sev[mf % 2] = b.prod("scalar", "s", lambda e, mf=mf, pb=pb: e.activation(
                    out=cacc[:, (mf % 2) * CH: (mf % 2) * CH + 3], in_=pb[:, 125:128],
                    func=AF.Identity, bias=c_bin[:, mf:mf + 1]))
                b.w("vector", sev[mf % 2])
                vv = b.prod("vector", "v", lambda e, mf=mf: e.tensor_scalar(
                    out=xbc16[:, mf * CVW: mf * CVW + 3],
                    in0=cacc[:, (mf % 2) * CH: (mf % 2) * CH + 3],
                    scalar1=c_hmask[:, 0:1], scalar2=None, op0=OP.mult))
                b.w("scalar", vv)
            b.prod("vector", "v", lambda e: (e.memset(sloc[:], 0.0),
                                             e.memset(smx[:, 0:8], 0.0),
                                             e.memset(smtpF[:], 0.0))[-1], 1)
            b.barrier()

            XC = [x1c, x1c2]
            slot_tok = [None, None]
            evtok = [None, None]
            sev = [None, None]
            sctok = [None, None]
            xwtok = [None, None]
            siltok = [None, None]
            xtrtok = [None, None]
            dgtok = [None, None]
            cvtok = [None, None]
            pseg_rd = None
            cacc16 = cacc[:].bitcast(F16)
            xnf_rt = None
            xn_rt = None
            szt_sp = None
            xtt_sp = None
            bfc_sp = None
            sl_sp = None
            halv = None
            for c in range(NCH):
                csfc = sm32[:, c * CH:(c + 1) * CH]
                dt16c = dt16a[0:32, c * CH:(c + 1) * CH]
                xc = XC[c % 2]
                if slot_tok[c % 2] is not None:
                    b.w("sync", slot_tok[c % 2])
                ldc = b.dma("sync", lambda e, c=c, xc=xc: [
                    e.dma_start(out=xc[:, 0:D], in_=x1_scr[c * CH: c * CH + 128, :]),
                    e.dma_start(out=xc[:, D:2 * D], in_=x1_scr[c * CH + 128:(c + 1) * CH, :])], 2)
                b.w("vector", ldc)
                b.w("scalar", ldc, xn_rt)
                for th in range(2):
                    layer_norm(xc[:, th * D:(th + 1) * D], xn2[:, th * D:(th + 1) * D],
                               lnt2, lnsq16b, lnc2_)
                lnS = b.now("s")
                slot_tok[c % 2] = lnS
                b.w("tensor", lnS)
                for kt in range(8):
                    pt = PT[kt % 2]
                    b.w("tensor", evtok[kt % 2])
                    def tr(e, kt=kt, pt=pt):
                        for th in range(2):
                            ins = e.transpose(pt[:, th * 128:(th + 1) * 128],
                                              xn2[:, th * D + kt * 128: th * D + (kt + 1) * 128],
                                              c_id16[:])
                        return ins
                    tt = b.prod("tensor", "t", tr)
                    b.w("vector", tt, xnf_rt)
                    evtok[kt % 2] = b.prod("vector", "v", lambda e, kt=kt, pt=pt: e.tensor_copy(
                        out=xnf2[:, kt * 256:(kt + 1) * 256], in_=pt[:, 0:256]))
                xn_rt = b.now("t")
                b.w("tensor", b.now("v"))

                b.w("scalar", szt_sp)
                for mt in range(2):
                    for nz in range(4):
                        qq = (mt * 4 + nz) % 2
                        pb = PMM[qq]
                        b.w("tensor", sev[qq], sctok[qq])
                        def mm(e, mt=mt, nz=nz, pb=pb):
                            for kt in range(8):
                                e.matmul(pb[:, 0:512],
                                         xnf2[:, kt * 256 + mt * 128: kt * 256 + (mt + 1) * 128],
                                         ar1[:, kt * DPROJ + nz * 512: kt * DPROJ + (nz + 1) * 512],
                                         start=(kt == 0), stop=False)
                            return e.matmul(pb[:, 0:512], c_repl16[:],
                                            rowmmF[:, nz * 512:(nz + 1) * 512],
                                            start=False, stop=True)
                        tmc = b.prod("tensor", "t", mm)
                        b.w("scalar", tmc)
                        sev[qq] = b.prod("scalar", "s", lambda e, mt=mt, nz=nz, pb=pb: e.activation(
                            out=szt[:, mt * DSSM + nz * 512: mt * DSSM + (nz + 1) * 512],
                            in_=pb[:, 0:512], func=AF.Silu))
                b.w("gpsimd", b.now("s"))
                szt_sp = b.dma("gpsimd", lambda e, c=c: e.dma_start(out=sz_scr[c], in_=szt[:]))

                b.w("scalar", halv)
                for mf in range(NXT):
                    pb = PMM[mf % 2]
                    b.w("tensor", sev[mf % 2])
                    def mm(e, mf=mf, pb=pb):
                        for kt in range(8):
                            ins = e.matmul(pb[:, 0:CH],
                                           ar1[:, kt * DPROJ + DSSM + mf * 128:
                                               kt * DPROJ + DSSM + (mf + 1) * 128],
                                           xnf2[:, kt * 256:(kt + 1) * 256],
                                           start=(kt == 0), stop=(kt == 7))
                        return ins
                    tmc = b.prod("tensor", "t", mm)
                    b.w("scalar", tmc)
                    sev[mf % 2] = b.prod("scalar", "s", lambda e, mf=mf, pb=pb: e.activation(
                        out=xbc16[:, mf * CVW + 3: mf * CVW + 3 + CH], in_=pb[:, 0:CH],
                        func=AF.Identity, bias=c_bin[:, mf:mf + 1]))
                pb = PMM[0]
                b.w("tensor", sev[0])
                tmc = b.prod("tensor", "t", lambda e, pb=pb: [
                    e.matmul(pb[0:32, 0:CH],
                             ar1[:, kt * DPROJ + DSSM + NXT * 128: kt * DPROJ + DSSM + NXT * 128 + 32],
                             xnf2[:, kt * 256:(kt + 1) * 256],
                             start=(kt == 0), stop=(kt == 7)) for kt in range(8)][-1])
                xnf_rt = b.now("t")
                tmc_dt = tmc

                b.w("tensor", b.now("s"), halv)
                for mf in range(NXT):
                    qq = mf % 2
                    dsl = cacc16[:, qq * 512: qq * 512 + 512]
                    pcv = pseg[:, 0:CH] if qq == 0 else pseg[:, 512:512 + CH]
                    b.w("gpsimd", cvtok[qq])
                    def dg(e, mf=mf, dsl=dsl):
                        for k in range(KCONV):
                            ins = e.tensor_tensor(
                                out=dsl[:, k * 128:(k + 1) * 128], in0=c_id16[:],
                                in1=c_convw[:, mf * KCONV + k: mf * KCONV + k + 1]
                                    .broadcast_to([128, 128]),
                                op=OP.mult)
                        return ins
                    dgtok[qq] = b.prod("gpsimd", "g", dg, KCONV)
                    b.w("tensor", dgtok[qq], siltok[qq])
                    if qq == 0:
                        b.w("tensor", pseg_rd)
                    def cmm(e, mf=mf, dsl=dsl, pcv=pcv):
                        for k in range(KCONV):
                            ins = e.matmul(pcv, dsl[:, k * 128:(k + 1) * 128],
                                           xbc16[:, mf * CVW + k: mf * CVW + k + CH],
                                           start=(k == 0), stop=(k == KCONV - 1))
                        return ins
                    cvtok[qq] = b.prod("tensor", "t", cmm)
                    b.w("scalar", cvtok[qq], xtrtok[qq])
                    if mf < 16:
                        b.w("scalar", xtt_sp)
                        siltok[qq] = b.prod("scalar", "s", lambda e, qq=qq, mf=mf, pcv=pcv: e.activation(
                            out=xsil[:, qq * CH: qq * CH + CH], in_=pcv,
                            func=AF.Silu, bias=c_convb[:, mf:mf + 1]))
                        b.w("tensor", siltok[qq], evtok[qq])
                        pt = PT[qq]
                        def tr(e, qq=qq, pt=pt):
                            for th in range(2):
                                ins = e.transpose(pt[:, th * 128:(th + 1) * 128],
                                                  xsil[:, qq * CH + th * 128: qq * CH + (th + 1) * 128],
                                                  c_id16[:])
                            return ins
                        tt = b.prod("tensor", "t", tr)
                        xtrtok[qq] = tt
                        b.w("vector", tt)
                        evtok[qq] = b.prod("vector", "v", lambda e, mf=mf, pt=pt: (
                            e.tensor_copy(out=xtt[:, mf * 128:(mf + 1) * 128], in_=pt[:, 0:128]),
                            e.tensor_copy(out=xtt[:, DSSM + mf * 128: DSSM + (mf + 1) * 128],
                                          in_=pt[:, 128:256]))[-1], 2)
                    else:
                        b.w("scalar", bfc_sp)
                        siltok[qq] = b.prod("scalar", "s", lambda e, mf=mf, pcv=pcv: e.activation(
                            out=bfc[:, (mf - 16) * CH: (mf - 15) * CH], in_=pcv,
                            func=AF.Silu, bias=c_convb[:, mf:mf + 1]))
                b.w("gpsimd", b.now("v"), b.now("s"))
                xtt_sp = b.dma("gpsimd", lambda e, c=c: e.dma_start(out=xt_scr[c], in_=xtt[:]))
                bfc_sp = b.dma("gpsimd", lambda e, c=c: e.dma_start(out=bc_scr[c], in_=bfc[:]))
                b.w("scalar", tmc_dt)
                sdt0 = b.prod("scalar", "s", lambda e, pb=PMM[0]: e.activation(
                    out=smtp[0:32, 0:CH], in_=pb[0:32, 0:CH], func=AF.Exp, bias=c_dtcomb[:, 0:1]))
                sev[0] = sdt0
                b.w("scalar", sdt0)
                sdt = b.prod("scalar", "s", lambda e: e.activation(
                    out=smtp[0:32, CH:2 * CH], in_=smtp[0:32, 0:CH], func=AF.Ln, bias=1.0))
                b.w("vector", sdt)
                vdt16 = b.prod("vector", "v", lambda e, dt16c=dt16c: e.tensor_copy(
                    out=dt16c, in_=smtp[0:32, CH:2 * CH]))
                b.w("vector", b.now("s"), b.now("v"), b.now("t"))
                halv = b.prod("vector", "v", lambda e: e.tensor_copy(
                    out=xbc16[:].rearrange("p (m w) -> p m w", w=CVW)[:, :, 0:3],
                    in_=xbc16[:].rearrange("p (m w) -> p m w", w=CVW)[:, :, CH:CH + 3]))

                vda = b.prod("vector", "v", lambda e: e.tensor_scalar(
                    out=smtp[0:32, 0:CH], in0=smtp[0:32, CH:2 * CH],
                    scalar1=c_acol[:, 0:1], scalar2=None, op0=OP.mult))
                cur, oth = 0, 1
                for kbit in range(8):
                    s_ = 1 << kbit
                    b.w("vector", b.now("v"))
                    b.prod("vector", "v", lambda e, s_=s_, cur=cur, oth=oth: (
                        e.tensor_copy(out=smtp[0:32, oth * CH: oth * CH + s_],
                                      in_=smtp[0:32, cur * CH: cur * CH + s_]),
                        e.tensor_tensor(out=smtp[0:32, oth * CH + s_: oth * CH + CH],
                                        in0=smtp[0:32, cur * CH + s_: cur * CH + CH],
                                        in1=smtp[0:32, cur * CH: cur * CH + CH - s_],
                                        op=OP.add))[-1], 2)
                    cur, oth = oth, cur
                b.w("vector", b.now("v"))
                vcs = b.prod("vector", "v", lambda e, csfc=csfc, cur=cur: e.tensor_copy(
                    out=csfc, in_=smtp[0:32, cur * CH: cur * CH + CH]))
                b.w("scalar", vcs)
                scd = b.prod("scalar", "s", lambda e, csfc=csfc: (
                    e.activation(out=cdcol, in_=csfc[:, CH - 1:CH], func=AF.Exp),
                    e.activation(out=wstf[0:32, 0:CH], in_=csfc, func=AF.Exp,
                                 scale=-1.0, bias=csfc[:, CH - 1:CH]))[-1], 2)
                b.w("vector", scd)
                vws0 = b.prod("vector", "v", lambda e, dt16c=dt16c: e.tensor_tensor(
                    out=wstf[0:32, 0:CH], in0=wstf[0:32, 0:CH], in1=dt16c, op=OP.mult))
                b.w("vector", vws0)
                vws = b.prod("vector", "v", lambda e: e.tensor_copy(
                    out=wstf16[0:32, :], in_=wstf[0:32, 0:CH]))
                b.w("tensor", scd, b.now("v"))
                tcd = b.prod("tensor", "t", lambda e: e.transpose(
                    pseg[0:1, 0:32], cdcol, c_id32[:]))
                b.w("vector", tcd)
                vc1 = b.prod("vector", "v", lambda e: e.tensor_copy(
                    out=smtp[0:1, 480:512], in_=pseg[0:1, 0:32]))
                pseg_rd = vc1
                b.w("tensor", vc1)
                trc = b.prod("tensor", "t", lambda e: e.matmul(
                    paux[:, 0:32], c_repl32[:], smtpF[:], start=True, stop=True))
                b.w("vector", trc)
                vrc = b.prod("vector", "v", lambda e: e.tensor_copy(out=repcd[:, 0:32], in_=paux[:, 0:32]))

                pt = PT[0]
                b.w("tensor", b.now("v"))
                tbt = b.prod("tensor", "t", lambda e, pt=pt: (
                    e.transpose(pt[:, 0:128], bfc[:, 0:128], c_id16[:]),
                    e.transpose(pt[:, 128:256], bfc[:, 128:256], c_id16[:]))[-1], 2)
                evtok[0] = None
                b.w("vector", tbt)
                vbt = b.prod("vector", "v", lambda e, pt=pt: e.tensor_copy(
                    out=btt[:], in_=pt[:, 0:256]))
                evtok[0] = vbt

                b.w("tensor", vws, vbt, b.now("v"))
                for lh in range(2):
                    for sl in range(4):
                        qq = sl % 2
                        slot = paux[:, qq * 512: qq * 512 + 512]
                        b.w("tensor", xwtok[qq])
                        tws = b.prod("tensor", "t", lambda e, lh=lh, sl=sl, slot=slot: e.matmul(
                            slot, wstf16[:, lh * 128:(lh + 1) * 128],
                            c_eexp[:, sl * 512:(sl + 1) * 512], start=True, stop=True))
                        b.w("vector", tws)
                        xwtok[qq] = b.prod("vector", "v",
                            lambda e, lh=lh, sl=sl, slot=slot: e.tensor_tensor(
                                out=xw[:, lh * DSSM + sl * 512: lh * DSSM + (sl + 1) * 512],
                                in0=xtt[:, lh * DSSM + sl * 512: lh * DSSM + (sl + 1) * 512],
                                in1=slot, op=OP.mult))
                b.w("tensor", b.now("v"), b.now("s"))
                b.w("vector", sl_sp)
                for sl in range(4):
                    qq = sl % 2
                    pb = PMM[qq]
                    b.w("tensor", sctok[qq])
                    def mst(e, sl=sl, pb=pb):
                        for lh in range(2):
                            ins = e.matmul(pb[:, 0:512], btt[:, lh * 128:(lh + 1) * 128],
                                           xw[:, lh * DSSM + sl * 512: lh * DSSM + (sl + 1) * 512],
                                           start=(lh == 0), stop=(lh == 1))
                        return ins
                    tst = b.prod("tensor", "t", mst)
                    b.w("vector", tst)
                    def scan(e, sl=sl, pb=pb):
                        for hh in range(8):
                            h = sl * 8 + hh
                            ins = e.scalar_tensor_tensor(
                                out=sloc[:, h * HD:(h + 1) * HD],
                                in0=sloc[:, h * HD:(h + 1) * HD],
                                scalar=repcd[:, h:h + 1], in1=pb[:, hh * HD:(hh + 1) * HD],
                                op0=OP.mult, op1=OP.add)
                        return ins
                    sctok[qq] = b.prod("vector", "v", scan, 8)
                if c < NCH - 1:
                    b.w("gpsimd", b.now("v"))
                    sl_sp = b.dma("gpsimd", lambda e, c=c: e.dma_start(out=sl_scr[c], in_=sloc[:]))
            b.barrier()

        def px():
            v0 = b.prod("vector", "v", lambda e: e.tensor_reduce(
                out=sumcs, in_=sm32[:].rearrange("p (c l) -> p l c", l=CH)[:, CH - 1:CH, :],
                axis=AX, op=OP.add))
            b.w("scalar", v0)
            s0 = b.prod("scalar", "s", lambda e: e.activation(
                out=pruncol, in_=sumcs, func=AF.Exp))
            b.w("tensor", s0)
            t0 = b.prod("tensor", "t", lambda e: e.transpose(
                pseg[0:1, 0:32], pruncol, c_id32[:]))
            b.w("vector", t0)
            v1 = b.prod("vector", "v", lambda e: e.tensor_copy(
                out=prun[:].rearrange("a (h p) -> a h p", p=HD),
                in_=pseg[0:1, 0:32].unsqueeze(2).broadcast_to([1, 32, HD])))
            b.w("sync", v1)
            d1 = b.dma("sync", lambda e: e.dma_start(out=cc_in[0:128, :], in_=sloc[:]))
            d2 = b.dma("sync", lambda e: e.dma_start(out=cc_in[128:129, :], in_=prun[:]))
            b.w("gpsimd", d1, d2)
            cc = b.prod("gpsimd", "g", lambda e: e.collective_compute(
                "AllGather", OP.bypass, ins=[cc_in], outs=[cc_out],
                replica_groups=[list(range(NCORES))]))
            b.w("sync", cc)
            def ldr(e):
                return [e.dma_start(out=gath[j:j + 1, 0:DSSM],
                                    in_=cc_out[(j + 1) * 129 + 128: (j + 1) * 129 + 129, :])
                        for j in range(7)]
            d3 = b.dma("sync", ldr, 7)
            b.w("vector", d3)
            vdm = b.prod("vector", "v", lambda e: e.tensor_scalar(
                out=gath[0:7, 0:DSSM], in0=gath[0:7, 0:DSSM],
                scalar1=c_mnext[0:7, 0:1], scalar2=c_imnext[0:7, 0:1],
                op0=OP.mult, op1=OP.add))
            vz = b.prod("vector", "v", lambda e: e.memset(carry[:], 0.0))
            prevj = vz
            for j in range(7):
                b.w("sync", vdm)
                dmj = b.dma("sync", lambda e, j=j: e.dma_start(
                    out=dmcur_row, in_=gath[j:j + 1, 0:DSSM]))
                dj = b.dma("sync", lambda e, j=j: e.dma_start(
                    out=sused[:], in_=cc_out[j * 129: j * 129 + 128, :]))
                b.w("vector", dj)
                for slpair in range(2):
                    b.w("tensor", dmj, prevj)
                    def rep2(e, slpair=slpair):
                        e.matmul(paux[:, 0:512], c_repl32[:],
                                 dmcurF[:, slpair * 1024: slpair * 1024 + 512],
                                 start=True, stop=True)
                        return e.matmul(paux[:, 512:1024], c_repl32[:],
                                        dmcurF[:, slpair * 1024 + 512:(slpair + 1) * 1024],
                                        start=True, stop=True)
                    trp = b.prod("tensor", "t", rep2)
                    b.w("vector", trp)
                    cb1 = b.prod("vector", "v", lambda e, j=j, slpair=slpair: [
                        e.scalar_tensor_tensor(
                            out=carry[:, (slpair * 2 + qq) * 512:(slpair * 2 + qq + 1) * 512],
                            in0=sused[:, (slpair * 2 + qq) * 512:(slpair * 2 + qq + 1) * 512],
                            scalar=c_mcols[:, j:j + 1],
                            in1=carry[:, (slpair * 2 + qq) * 512:(slpair * 2 + qq + 1) * 512],
                            op0=OP.mult, op1=OP.add) for qq in range(2)][-1], 2)
                    b.w("vector", cb1)
                    prevj = b.prod("vector", "v", lambda e, slpair=slpair: [
                        e.tensor_tensor(
                            out=carry[:, (slpair * 2 + qq) * 512:(slpair * 2 + qq + 1) * 512],
                            in0=carry[:, (slpair * 2 + qq) * 512:(slpair * 2 + qq + 1) * 512],
                            in1=paux[:, qq * 512:(qq + 1) * 512], op=OP.mult) for qq in range(2)][-1], 2)
                b.w("sync", prevj)
            b.prod("vector", "v", lambda e: e.memset(smx[:, 1:2], 0.0))
            b.barrier()

        def p3():
            b.dma("sync", lambda e: e.dma_start(out=ar1[:, 0:16 * D], in_=woutT_scr))
            dbo = b.dma("sync", lambda e: e.dma_start(out=rowmm[0:1, 0:D], in_=bout_row))
            b.w("tensor", dbo)
            tbo = b.prod("tensor", "t", lambda e: (
                e.matmul(paux[:, 0:512], c_repl16[:], rowmmF[:, 0:512], start=True, stop=True),
                e.matmul(paux[:, 512:1024], c_repl16[:], rowmmF[:, 512:1024], start=True, stop=True))[-1], 1)
            b.w("vector", tbo)
            b.prod("vector", "v", lambda e: (
                e.tensor_copy(out=boutrep[:, 0:512], in_=paux[:, 0:512]),
                e.tensor_copy(out=boutrep[:, 512:1024], in_=paux[:, 512:1024]))[-1], 1)
            b.prod("vector", "v", lambda e: (e.memset(bdseg, 0.0),
                                             e.memset(bdlhs, 0.0))[-1], 1)
            b.prod("tensor", "t", lambda e: e.matmul(
                pseg[:, 0:NH], c_ones[:], c_drow[:], start=True, stop=True))
            b.w("vector", b.now("t"))
            b.prod("vector", "v", lambda e: e.tensor_copy(out=tmp3[:, 0:NH], in_=pseg[:, 0:NH]))
            b.w("scalar", b.now("v"))
            def dis(e):
                for h in range(NH):
                    ins = e.activation(out=di16[:, h * 128:(h + 1) * 128], in_=c_id16[:],
                                       func=AF.Copy, scale=tmp3[:, h:h + 1])
                return ins
            b.prod("scalar", "s", dis)
            b.barrier()

            prev_t = None
            prev_v = None
            x2_sp = None
            dsc = None
            for c in range(NCH):
                csfc = sm32[:, c * CH:(c + 1) * CH]
                dt16c = dt16a[0:32, c * CH:(c + 1) * CH]
                dt16cF = dt16a[:, c * CH:(c + 1) * CH]
                b.w("sync", prev_t, prev_v, x2_sp)
                ldc = b.dma("sync", lambda e, c=c: [
                    e.dma_start(out=xtt[:], in_=xt_scr[c]),
                    e.dma_start(out=szt[:], in_=sz_scr[c]),
                    e.dma_start(out=bfc[:], in_=bc_scr[c]),
                    e.dma_start(out=x1c3[:, 0:D], in_=x1_scr[c * CH: c * CH + 128, :]),
                    e.dma_start(out=x1c3[:, D:2 * D], in_=x1_scr[c * CH + 128:(c + 1) * CH, :])], 5)
                ldp = None
                if c > 0:
                    ldp = b.dma("sync", lambda e, c=c: e.dma_start(out=slpf[:], in_=sl_scr[c - 1]))
                if c == 0:
                    b.w("vector", b.now("t"))
                    b.prod("vector", "v", lambda e: e.tensor_copy(out=sused[:], in_=carry[:]))
                else:
                    b.w("vector", b.now("v"))
                    va = b.prod("vector", "v", lambda e, c=c: e.tensor_tensor(
                        out=sumcs, in0=sumcs, in1=sm32[:, (c - 1) * CH + CH - 1:(c - 1) * CH + CH],
                        op=OP.add))
                    b.w("scalar", va)
                    sp = b.prod("scalar", "s", lambda e: e.activation(
                        out=pruncol, in_=sumcs, func=AF.Exp))
                    b.w("tensor", sp)
                    tp_ = b.prod("tensor", "t", lambda e: e.transpose(
                        pseg[0:1, 0:32], pruncol, c_id32[:]))
                    b.w("vector", tp_)
                    vp = b.prod("vector", "v", lambda e: e.tensor_copy(
                        out=prun[:].rearrange("a (h p) -> a h p", p=HD),
                        in_=pseg[0:1, 0:32].unsqueeze(2).broadcast_to([1, 32, HD])))
                    b.w("vector", ldp)
                    b.w("tensor", vp)
                    for slpair in range(2):
                        b.w("tensor", b.now("v"))
                        def rep2(e, slpair=slpair):
                            e.matmul(paux[:, 0:512], c_ones[:],
                                     prun[:, slpair * 1024: slpair * 1024 + 512],
                                     start=True, stop=True)
                            return e.matmul(paux[:, 512:1024], c_ones[:],
                                            prun[:, slpair * 1024 + 512:(slpair + 1) * 1024],
                                            start=True, stop=True)
                        trp = b.prod("tensor", "t", rep2)
                        b.w("vector", trp)
                        cb1 = b.prod("vector", "v", lambda e, slpair=slpair: [
                            e.tensor_tensor(
                                out=sused[:, (slpair * 2 + qq) * 512:(slpair * 2 + qq + 1) * 512],
                                in0=carry[:, (slpair * 2 + qq) * 512:(slpair * 2 + qq + 1) * 512],
                                in1=paux[:, qq * 512:(qq + 1) * 512], op=OP.mult) for qq in range(2)][-1], 2)
                        b.w("vector", cb1)
                        b.prod("vector", "v", lambda e, slpair=slpair: [
                            e.tensor_tensor(
                                out=sused[:, (slpair * 2 + qq) * 512:(slpair * 2 + qq + 1) * 512],
                                in0=sused[:, (slpair * 2 + qq) * 512:(slpair * 2 + qq + 1) * 512],
                                in1=slpf[:, (slpair * 2 + qq) * 512:(slpair * 2 + qq + 1) * 512],
                                op=OP.add) for qq in range(2)][-1], 2)

                b.w("vector", b.now("v"))
                vs16 = b.prod("vector", "v", lambda e: e.tensor_copy(out=sused16[:], in_=sused[:]))
                b.w("tensor", ldc, b.now("v"))
                tg = b.prod("tensor", "t", lambda e: [
                    e.matmul(pseg[:, jh * 512: jh * 512 + CH],
                             bfc[:, jh * 128:(jh + 1) * 128],
                             bfc[:, 256:512],
                             start=True, stop=True) for jh in range(2)][-1])
                b.w("vector", tg)
                vg = b.prod("vector", "v", lambda e: (
                    e.tensor_tensor(out=gmm[:, 0:CH], in0=pseg[:, 0:CH],
                                    in1=c_tril[:, 0:CH], op=OP.mult),
                    e.tensor_tensor(out=gmm[:, CH:2 * CH], in0=pseg[:, 512:512 + CH],
                                    in1=c_tril[:, CH:2 * CH], op=OP.mult))[-1], 2)

                vhi = b.prod("vector", "v", lambda e, csfc=csfc: e.tensor_copy(
                    out=cshl[0:32, 0:CH], in_=csfc))
                b.w("vector", vhi)
                vlo = b.prod("vector", "v", lambda e, csfc=csfc: e.tensor_tensor(
                    out=smtp[0:32, 0:CH], in0=csfc, in1=cshl[0:32, 0:CH], op=OP.subtract))
                b.w("vector", vlo)
                vhl = b.prod("vector", "v", lambda e: (
                    e.tensor_copy(out=cshl[0:32, CH:2 * CH], in_=smtp[0:32, 0:CH]),
                    e.tensor_scalar(out=ncshl[0:32, 0:CH], in0=cshl[0:32, 0:CH], scalar1=-1.0,
                                    scalar2=None, op0=OP.mult),
                    e.tensor_scalar(out=ncshl[0:32, CH:2 * CH], in0=smtp[0:32, 0:CH], scalar1=-1.0,
                                    scalar2=None, op0=OP.mult))[-1], 3)
                sec = b.prod("scalar", "s", lambda e, csfc=csfc: e.activation(
                    out=ecsf16[0:32, :], in_=csfc, func=AF.Exp))
                b.w("gpsimd", vhl, sec, b.now("t"), dsc if c > 0 else None)
                def scatw(e):
                    return [e.dma_start(out=bd_scr[0:32, :], in_=cshl[0:32, :]),
                            e.dma_start(out=bd_scr[32:64, :], in_=ncshl[0:32, :])]
                dscw = b.dma("gpsimd", scatw, 2)
                b.w("gpsimd", dscw)
                def scatr(e):
                    lst = []
                    for l in range(2):
                        for p in range(4):
                            lst.append(e.dma_start(
                                out=bdseg[l * 4 + p: l * 4 + p + 1, :]
                                    .rearrange("a (g c2) -> a g c2", g=8)[:, :, p * CH:(p + 1) * CH],
                                in_=bd_scr[0:32, :].rearrange("(g p) (l c2) -> p g l c2", p=4, l=2)
                                    [p:p + 1, :, l:l + 1, :].squeeze(2)))
                        lst.append(e.dma_start(
                            out=bdlhs[l * 4:(l + 1) * 4, :].rearrange("p (g j c2) -> p g j c2", g=8, j=2),
                            in_=bd_scr[32:64, :].rearrange("(g p) (l j c2) -> p g l j c2", p=4, l=2, j=2)
                                [:, :, l:l + 1, :, :].squeeze(2)))
                    return lst
                dsc = b.dma("gpsimd", scatr, 10)

                b.w("tensor", b.now("v"), sec)
                xwtok = [None, None]
                for lh in range(2):
                    for sl in range(4):
                        qq = sl % 2
                        slot = paux[:, qq * 512: qq * 512 + 512]
                        b.w("tensor", xwtok[qq])
                        tws = b.prod("tensor", "t", lambda e, lh=lh, sl=sl, slot=slot, dt16cF=dt16cF: e.matmul(
                            slot, dt16cF[:, lh * 128:(lh + 1) * 128],
                            c_eexp[:, sl * 512:(sl + 1) * 512], start=True, stop=True))
                        b.w("vector", tws)
                        xwtok[qq] = b.prod("vector", "v",
                            lambda e, lh=lh, sl=sl, slot=slot: e.tensor_tensor(
                                out=xw[:, lh * DSSM + sl * 512: lh * DSSM + (sl + 1) * 512],
                                in0=xtt[:, lh * DSSM + sl * 512: lh * DSSM + (sl + 1) * 512],
                                in1=slot, op=OP.mult))
                for ih in range(2):
                    for sl in range(4):
                        qq = sl % 2
                        slot = paux[:, qq * 512: qq * 512 + 512]
                        b.w("tensor", xwtok[qq])
                        tws = b.prod("tensor", "t", lambda e, ih=ih, sl=sl, slot=slot: e.matmul(
                            slot, ecsf16[:, ih * 128:(ih + 1) * 128],
                            c_eexp[:, sl * 512:(sl + 1) * 512], start=True, stop=True))
                        b.w("vector", tws)
                        xwtok[qq] = b.prod("vector", "v",
                            lambda e, ih=ih, sl=sl, slot=slot: e.tensor_tensor(
                                out=szex[:, ih * DSSM + sl * 512: ih * DSSM + (sl + 1) * 512],
                                in0=szt[:, ih * DSSM + sl * 512: ih * DSSM + (sl + 1) * 512],
                                in1=slot, op=OP.mult))

                b.w("tensor", dsc, b.now("v"))
                ytok = None
                mrd = [None, None]
                segrd = None
                gdone = {}
                PSEG2 = [pseg, paux]
                for g in range(9):
                    if g < 8:
                        mslot = mtt2[g % 2]
                        for jh in range(2):
                            ps = PSEG2[jh]
                            b.w("tensor", mrd[g % 2], segrd)
                            def sgm(e, g=g, jh=jh, ps=ps):
                                for nd in range(2):
                                    e.matmul(ps[:, nd * 512:(nd + 1) * 512], c_segones[:],
                                             bdsegF[:, g * 1024 + nd * 512: g * 1024 + (nd + 1) * 512],
                                             start=True, stop=False)
                                    ins = e.matmul(ps[:, nd * 512:(nd + 1) * 512],
                                                   bdlhsF[:, (g * 2 + jh) * 128:(g * 2 + jh + 1) * 128],
                                                   c_bdones[:, nd * 512:(nd + 1) * 512],
                                                   start=False, stop=True)
                                return ins
                            tsg = b.prod("tensor", "t", sgm)
                            b.w("vector", tsg, mrd[g % 2])
                            vmn = b.prod("vector", "v", lambda e, ps=ps, mslot=mslot, jh=jh: (
                                e.tensor_scalar(out=mslot[:, jh * 1024: jh * 1024 + 512],
                                                in0=ps[:, 0:512], scalar1=0.0, scalar2=None, op0=OP.min),
                                e.tensor_scalar(out=mslot[:, jh * 1024 + 512: jh * 1024 + 1024],
                                                in0=ps[:, 512:1024], scalar1=0.0, scalar2=None, op0=OP.min))[-1], 2)
                            segrd = vmn
                            b.w("scalar", vmn)
                            sxp = b.prod("scalar", "s", lambda e, mslot=mslot, jh=jh: e.activation(
                                out=mslot[:, jh * 1024:(jh + 1) * 1024],
                                in_=mslot[:, jh * 1024:(jh + 1) * 1024], func=AF.Exp))
                            b.w("gpsimd", sxp, vg)
                            mtok = b.prod("gpsimd", "g", lambda e, mslot=mslot, jh=jh: e.tensor_tensor(
                                out=mslot[:, jh * 1024:(jh + 1) * 1024].rearrange("p (r c2) -> p r c2", r=4),
                                in0=mslot[:, jh * 1024:(jh + 1) * 1024].rearrange("p (r c2) -> p r c2", r=4),
                                in1=gmm[:, jh * CH:(jh + 1) * CH].unsqueeze(1).broadcast_to([128, 4, CH]),
                                op=OP.mult))
                        gdone[g] = mtok
                    if g >= 1:
                        gp = g - 1
                        mslot = mtt2[gp % 2]
                        b.w("tensor", gdone[gp], ytok)
                        def ymm(e, gp=gp, mslot=mslot):
                            for ih in range(2):
                                for hh in range(4):
                                    h = gp * 4 + hh
                                    col = pmm0[:, ih * 256 + hh * HD: ih * 256 + (hh + 1) * HD]
                                    for jh in range(2):
                                        e.matmul(col,
                                                 mslot[:, jh * 1024 + hh * CH + ih * 128:
                                                     jh * 1024 + hh * CH + ih * 128 + 128],
                                                 xw[:, jh * DSSM + h * HD: jh * DSSM + (h + 1) * HD],
                                                 start=(jh == 0), stop=False)
                                    ins = e.matmul(col, di16[:, h * 128:(h + 1) * 128],
                                                   xtt[:, ih * DSSM + h * HD: ih * DSSM + (h + 1) * HD],
                                                   start=False, stop=True)
                            return ins
                        tym = b.prod("tensor", "t", ymm)
                        def yof(e, gp=gp):
                            for ih in range(2):
                                for hh in range(4):
                                    h = gp * 4 + hh
                                    ins = e.matmul(pmm1[:, ih * 256 + hh * HD: ih * 256 + (hh + 1) * HD],
                                                   bfc[:, 256 + ih * 128: 256 + (ih + 1) * 128],
                                                   sused16[:, h * HD:(h + 1) * HD],
                                                   start=True, stop=True)
                            return ins
                        tyo = b.prod("tensor", "t", yof)
                        mrd[gp % 2] = tyo
                        b.w("vector", tym, tyo)
                        for ih in range(2):
                            b.w("vector", b.now("v"))
                            gm1 = b.prod("vector", "v", lambda e, gp=gp, ih=ih: (
                                e.tensor_tensor(out=tmp3[:, 0:256],
                                                in0=pmm1[:, ih * 256:(ih + 1) * 256],
                                                in1=szex[:, ih * DSSM + gp * 256: ih * DSSM + (gp + 1) * 256],
                                                op=OP.mult),
                                e.tensor_tensor(out=tmp3[:, 256:512],
                                                in0=pmm0[:, ih * 256:(ih + 1) * 256],
                                                in1=szt[:, ih * DSSM + gp * 256: ih * DSSM + (gp + 1) * 256],
                                                op=OP.mult))[-1], 2)
                            b.w("vector", gm1)
                            ytok = b.prod("vector", "v", lambda e, gp=gp, ih=ih: e.tensor_tensor(
                                out=yt[:, ih * DSSM + gp * 256: ih * DSSM + (gp + 1) * 256],
                                in0=tmp3[:, 0:256], in1=tmp3[:, 256:512], op=OP.add))
                ytAll = b.now("v")

                for ih in range(2):
                    b.w("scalar", ytAll)
                    ssq = b.prod("scalar", "s", lambda e, ih=ih: e.activation(
                        out=xw[:, ih * DSSM:(ih + 1) * DSSM],
                        in_=yt[:, ih * DSSM:(ih + 1) * DSSM], func=AF.Square,
                        accum_out=rsc3[:, 0:1]))
                    b.w("vector", ssq)
                    vrs = b.prod("vector", "v", lambda e: e.tensor_scalar(
                        out=rsc3[:, 1:2], in0=rsc3[:, 0:1], scalar1=1.0 / DSSM,
                        scalar2=RMS_EPS, op0=OP.mult, op1=OP.add))
                    b.w("scalar", vrs)
                    srt = b.prod("scalar", "s", lambda e: e.activation(
                        out=rsc3[:, 2:3], in_=rsc3[:, 1:2], func=AF.Ln))
                    b.w("scalar", srt)
                    vrc = b.prod("scalar", "s", lambda e: e.activation(
                        out=rsc3[:, 3:4], in_=rsc3[:, 2:3], func=AF.Exp, scale=-0.5))
                    evtok2 = [None, None]
                    b.w("tensor", b.now("s"))
                    for half in range(4):
                        pt = PT[half % 2]
                        b.w("tensor", evtok2[half % 2])
                        def tr(e, ih=ih, half=half, pt=pt):
                            for qq in range(4):
                                kt = half * 4 + qq
                                ins = e.transpose(pt[:, qq * 128:(qq + 1) * 128],
                                                  yt[:, ih * DSSM + kt * 128: ih * DSSM + (kt + 1) * 128],
                                                  c_id16[:])
                            return ins
                        tt = b.prod("tensor", "t", tr)
                        b.w("vector", tt)
                        evtok2[half % 2] = b.prod("vector", "v",
                            lambda e, half=half, pt=pt: e.tensor_copy(
                                out=yf[:, half * 512:(half + 1) * 512], in_=pt[:, 0:512]))
                    b.w("tensor", b.now("v"))
                    b.w("vector", vrc)
                    for nd in range(2):
                        pb = PMM[nd % 2]
                        def mmo(e, nd=nd, pb=pb):
                            for kt in range(16):
                                ins = e.matmul(pb[:, 0:512], yf[:, kt * 128:(kt + 1) * 128],
                                               ar1[:, kt * D + nd * 512: kt * D + (nd + 1) * 512],
                                               start=(kt == 0), stop=(kt == 15))
                            return ins
                        tmo = b.prod("tensor", "t", mmo)
                        b.w("scalar", tmo, vrc)
                        sat = b.prod("scalar", "s", lambda e, nd=nd, pb=pb: e.activation(
                            out=att[:, nd * 512:(nd + 1) * 512], in_=pb[:, 0:512],
                            func=AF.Copy, scale=rsc3[:, 3:4]))
                        b.w("tensor", sat)
                    b.w("vector", b.now("s"))
                    vxb = b.prod("vector", "v", lambda e: e.tensor_tensor(
                        out=att[:, 0:D], in0=att[:, 0:D], in1=boutrep[:], op=OP.add))
                    b.w("vector", vxb)
                    vx2 = b.prod("vector", "v", lambda e, ih=ih: e.tensor_tensor(
                        out=x1c3[:, ih * D:(ih + 1) * D], in0=att[:, 0:D],
                        in1=x1c3[:, ih * D:(ih + 1) * D], op=OP.add))
                    b.w("scalar", vx2)
                b.w("gpsimd", b.now("v"))
                x2_sp = b.dma("gpsimd", lambda e, c=c: [
                    e.dma_start(out=x2_scr[c * CH: c * CH + 128, :], in_=x1c3[:, 0:D]),
                    e.dma_start(out=x2_scr[c * CH + 128:(c + 1) * CH, :], in_=x1c3[:, D:2 * D])], 2)
                prev_t = b.now("t")
                prev_v = b.now("v")
            b.barrier()

        stop = DEBUG["stop"]
        p0()
        if stop != "p0":
            ffn_phase(u, x1_scr, with_halo=True)
        if stop not in ("p0", "p1"):
            p2()
        if stop not in ("p0", "p1", "p2"):
            px()
        if stop not in ("p0", "p1", "p2", "px"):
            p3()
        if stop is None:
            b.w("sync", DT["w1scr"], DT["w2scr"])
            b.dma("sync", lambda e: e.dma_start(out=ar1[:, 0:8 * HF], in_=w1t_scr))
            b.dma("sync", lambda e: e.dma_start(out=arq[:], in_=w2q_scr))
            d4 = b.dma("sync", lambda e: e.dma_start(out=wstg[0:1, 0:D], in_=b2_row))
            b.w("vector", d4)
            v4a = b.prod("vector", "v", lambda e: e.tensor_copy(out=lnt[0:1, 0:D].bitcast(F16)[0:1, 0:D], in_=wstg[0:1, 0:D]))
            b.w("vector", v4a)
            v4b = b.prod("vector", "v", lambda e: e.tensor_copy(out=wstg[0:1, 0:D], in_=lnt[0:1, 0:D].bitcast(F16)[0:1, 0:D]))
            b.w("tensor", v4b)
            t4 = b.prod("tensor", "t", lambda e: (
                e.matmul(paux[:, 0:512], c_ones[:], wstg[0:1, 0:512], start=True, stop=True),
                e.matmul(paux[:, 512:1024], c_ones[:], wstg[0:1, 512:1024], start=True, stop=True))[-1], 2)
            b.w("vector", t4)
            b.prod("vector", "v", lambda e: (
                e.tensor_copy(out=c_b2rep[:, 0:512], in_=paux[:, 0:512]),
                e.tensor_copy(out=c_b2rep[:, 512:1024], in_=paux[:, 512:1024]))[-1], 2)
            b.barrier()
            ffn_phase(x2_scr, out, with_halo=False)
        b.barrier()
        block = E(nc.Block())
        b.finalize(block)
    return nc, inp


def _host_prep(inputs):
    u_full = np.ascontiguousarray(np.asarray(inputs["L0_attn_u"], np.float32).reshape(16384, D))
    W_in = np.ascontiguousarray(np.asarray(inputs["L0_attn_W_in"], np.float32))
    b_in = np.asarray(inputs["L0_attn_b_in"], np.float32)
    conv_w = np.asarray(inputs["L0_attn_conv_w"], np.float32)
    conv_b = np.asarray(inputs["L0_attn_conv_b"], np.float32)
    dt_bias = np.asarray(inputs["L0_attn_dt_bias"], np.float32)
    A_log = np.asarray(inputs["L0_attn_A_log"], np.float32)
    Dv = np.asarray(inputs["L0_attn_D"], np.float32)
    norm_w = np.asarray(inputs["L0_attn_norm_w"], np.float32)
    W_out = np.ascontiguousarray(np.asarray(inputs["L0_attn_W_out"], np.float32))
    b_out = np.asarray(inputs["L0_attn_b_out"], np.float32)
    w1 = np.ascontiguousarray(np.asarray(inputs["L0_ffn_w1"], np.float32))
    b1 = np.asarray(inputs["L0_ffn_b1"], np.float32)
    w2 = np.ascontiguousarray(np.asarray(inputs["L0_ffn_w2"], np.float32))
    b2 = np.asarray(inputs["L0_ffn_b2"], np.float32)

    b1_col = np.ascontiguousarray(b1.reshape(HF // 128, 128).T)
    binx = b_in[DSSM:DPROJ]
    bin_col = np.zeros((128, NXT + 1), np.float32)
    for mf in range(NXT):
        bin_col[:, mf] = binx[mf * 128:(mf + 1) * 128]
    bin_col[0:32, NXT] = binx[NXT * 128: NXT * 128 + 32]
    convw_col = np.zeros((128, NXT * KCONV), np.float32)
    convb_col = np.zeros((128, NXT), np.float32)
    for mf in range(NXT):
        for k in range(KCONV):
            convw_col[:, mf * KCONV + k] = conv_w[mf * 128:(mf + 1) * 128, k]
        convb_col[:, mf] = conv_b[mf * 128:(mf + 1) * 128]
    normw_col = np.ascontiguousarray(norm_w.reshape(DSSM // 128, 128).T)

    trilT = np.zeros((128, 2 * CH), np.float32)
    for jh in range(2):
        for p in range(128):
            trilT[p, jh * CH:(jh + 1) * CH] = ((jh * 128 + p) <= np.arange(CH))
    eexp = np.zeros((128, DSSM), np.float16)
    for h in range(NH):
        eexp[h, h * HD:(h + 1) * HD] = 1.0
    bdones = np.zeros((128, 4 * CH), np.float16)
    for p in range(4):
        bdones[p, p * CH:(p + 1) * CH] = 1.0
        bdones[4 + p, p * CH:(p + 1) * CH] = 1.0
    segones = np.zeros((128, 128), np.float16)
    segones[0:8, :] = 1.0
    repl16 = np.zeros((128, 128), np.float16)
    repl16[0, :] = 1.0
    repl32 = np.zeros((128, 128), np.float32)
    repl32[0, :] = 1.0

    w1t_col = np.ascontiguousarray(
        w1.T.reshape(8, 128, HF).transpose(1, 0, 2).reshape(128, 8 * HF))
    w2t_col = np.ascontiguousarray(
        w2.T.reshape(32, 128, D).transpose(1, 0, 2).reshape(128, 32 * D))
    wint_col = np.ascontiguousarray(
        W_in.T.reshape(8, 128, DPROJ).transpose(1, 0, 2).reshape(128, 8 * DPROJ))
    woutt_col = np.ascontiguousarray(
        W_out.T.reshape(16, 128, D).transpose(1, 0, 2).reshape(128, 16 * D))

    base = dict(
        wint=wint_col, w1t=w1t_col, w2t=w2t_col, woutt=woutt_col,
        b1_col=b1_col, bin_col=bin_col,
        bz_row=np.ascontiguousarray(b_in[0:DSSM].reshape(1, DSSM)).astype(np.float16),
        b2_row=np.ascontiguousarray(b2.reshape(1, D)),
        bout_row=np.ascontiguousarray(b_out.reshape(1, D)).astype(np.float16),
        convw_col=convw_col, convb_col=convb_col,
        dtbias_col=np.ascontiguousarray(dt_bias.reshape(32, 1)),
        alog_col=np.ascontiguousarray(A_log.reshape(32, 1)),
        d_row=np.ascontiguousarray(Dv.reshape(1, NH)), normw_col=normw_col,
        ident16=np.eye(128).astype(np.float16), ident32=np.eye(32).astype(np.float32),
        ones_row=np.ones((1, 128), np.float32),
        ones16=np.ones((16, 128), np.float16),
        trilT=trilT.astype(np.float16), eexp=eexp, bdones=bdones,
        segones=segones, repl16=repl16, repl32=repl32,
    )
    in_maps = []
    for c in range(NCORES):
        m = dict(base)
        m["u"] = u_full[c * T:(c + 1) * T]
        m["u_halo"] = (np.zeros((128, D), np.float32) if c == 0
                       else np.ascontiguousarray(u_full[c * T - 128: c * T]))
        mc = np.zeros((128, NCORES), np.float32)
        mc[:, 0:c] = 1.0
        m["mcols"] = mc
        m["imcols"] = 1.0 - mc
        mn = np.zeros((128, 1), np.float32)
        for p in range(7):
            mn[p, 0] = 1.0 if (p + 1) < c else 0.0
        m["mnext"] = mn
        m["imnext"] = 1.0 - mn
        m["halo_mask"] = np.full((128, 1), 0.0 if c == 0 else 1.0, np.float32)
        in_maps.append(m)
    return in_maps


def kernel(**inputs):
    in_maps = _host_prep(inputs)
    nc, _ = build_nc()
    res = run_bass_kernel_spmd(nc, in_maps, core_ids=list(range(NCORES)))
    parts = [np.asarray(res.results[c]["out"]) for c in range(NCORES)]
    return np.concatenate(parts, axis=0).reshape(1, 16384, D).astype(np.float32)


# version 08
# speedup vs baseline: 1.4528x; 1.0521x over previous
import numpy as np
import ml_dtypes
import concourse.bass as bass
import concourse.mybir as mybir
from concourse.bass_utils import run_bass_kernel_spmd
from contextlib import ExitStack

F32 = mybir.dt.float32
F16 = mybir.dt.float16
FP8 = mybir.dt.float8e4
AF = mybir.ActivationFunctionType
OP = mybir.AluOpType
AX = mybir.AxisListType.X

NCORES = 8
T = 2048
CH = 256
NCH = T // CH
D = 1024
DSSM = 2048
NH = 32
HD = 64
DPROJ = 4384
HF = 4096
KCONV = 4
NXT = 18
CVW = CH + 3
LN_EPS = 1e-5
RMS_EPS = 1e-12

ENGS = ("sync", "gpsimd", "vector", "scalar", "tensor")

DEBUG = {"stop": None}


NDS = 12
NDG = 4


class Bld:
    def __init__(self, nc, ctx):
        self.nc = nc
        self.q = {e: [] for e in ENGS}
        self.sem = {}
        keys = (["v", "s", "t", "g"] + ["d%d" % i for i in range(NDS)]
                + ["e%d" % i for i in range(NDG)] + ["x0", "x1"])
        for k in keys:
            self.sem[k] = ctx.enter_context(nc.semaphore("sem_" + k))
        self.cnt = {k: 0 for k in self.sem}
        self.waited = {e: {} for e in ENGS}
        self.dsrot = 0
        self.dgrot = 0

    def prod(self, eng, key, f, n=1):
        self.cnt[key] += n
        sem = self.sem[key]
        def g(e):
            f(e).then_inc(sem, n)
        self.q[eng].append(g)
        return (key, self.cnt[key])

    def dma(self, eng, f, n=1, key=None):
        if key is not None:
            pass
        elif eng == "gpsimd":
            key = "e%d" % (self.dgrot % NDG)
            self.dgrot += 1
        elif True:
            key = "d%d" % (self.dsrot % NDS)
            self.dsrot += 1
        self.cnt[key] += 16 * n
        sem = self.sem[key]
        def g(e):
            r = f(e)
            lst = list(r) if isinstance(r, (list, tuple)) else [r]
            assert len(lst) == n, (len(lst), n)
            for i in lst:
                i.then_inc(sem, 16)
        self.q[eng].append(g)
        return (key, self.cnt[key])

    def w(self, eng, *toks):
        for tk in toks:
            if tk is None:
                continue
            key, val = tk
            if val <= self.waited[eng].get(key, 0):
                continue
            self.waited[eng][key] = val
            sem = self.sem[key]
            self.q[eng].append(lambda e, sem=sem, val=val: e.wait_ge(sem, val))

    def now(self, key):
        return (key, self.cnt[key])

    def barrier(self):
        snap = [(k, v) for k, v in self.cnt.items()]
        for eng in ENGS:
            self.w(eng, *snap)

    def finalize(self, block):
        for eng in ENGS:
            fs = self.q[eng]
            def runner(e, fs=fs):
                for f in fs:
                    f(e)
            getattr(block, eng)(runner)


def build_nc():
    nc = bass.Bass("TRN2", target_bir_lowering=False, debug=False)
    inp = {}

    def I(name, shape, dt=F32):
        inp[name] = nc.dram_tensor(name, list(shape), dt, kind="ExternalInput").ap()
        return inp[name]

    u = I("u", [T, D])
    u_halo = I("u_halo", [128, D])
    wint_h = I("wint", [128, 8 * DPROJ])
    w1t_h = I("w1t", [128, 8 * HF])
    w2t_h = I("w2t", [128, 32 * 1024])
    woutt_h = I("woutt", [128, 16 * D])
    b1_col = I("b1_col", [128, HF // 128])
    bin_col = I("bin_col", [128, NXT + 1])
    bz_row = I("bz_row", [1, DSSM], F16)
    b2_row = I("b2_row", [1, D])
    bout_row = I("bout_row", [1, D], F16)
    convw_col = I("convw_col", [128, NXT * KCONV])
    convb_col = I("convb_col", [128, NXT])
    dtbias_col = I("dtbias_col", [32, 1])
    alog_col = I("alog_col", [32, 1])
    d_row = I("d_row", [1, NH])
    normw_col = I("normw_col", [128, DSSM // 128])
    ident16 = I("ident16", [128, 128], F16)
    ident32 = I("ident32", [32, 32])
    ones_row = I("ones_row", [1, 128])
    ones16 = I("ones16", [16, 128], F16)
    trilT = I("trilT", [128, 2 * CH], F16)
    eexp = I("eexp", [128, DSSM], F16)
    bdones = I("bdones", [128, 4 * CH], F16)
    segones = I("segones", [128, 128], F16)
    repl16 = I("repl16", [128, 128], F16)
    repl32 = I("repl32", [128, 128])
    mcols = I("mcols", [128, NCORES])
    imcols = I("imcols", [128, NCORES])
    mnext = I("mnext", [128, 1])
    imnext = I("imnext", [128, 1])
    halo_mask = I("halo_mask", [128, 1])

    out = nc.dram_tensor("out", [T, D], F32, kind="ExternalOutput").ap()

    x1_scr = nc.dram_tensor("x1_scr", [T, D], F32, kind="Internal").ap()
    x2_scr = nc.dram_tensor("x2_scr", [T, D], F32, kind="Internal").ap()
    xt_scr = nc.dram_tensor("xt_scr", [NCH, 128, 2 * DSSM], F16, kind="Internal").ap()
    sz_scr = nc.dram_tensor("sz_scr", [NCH, 128, 2 * DSSM], F16, kind="Internal").ap()
    sl_scr = nc.dram_tensor("sl_scr", [NCH - 1, 128, DSSM], F32, kind="Internal").ap()
    bc_scr = nc.dram_tensor("bc_scr", [NCH, 128, 2 * CH], F16, kind="Internal").ap()
    bd_scr = nc.dram_tensor("bd_scr", [64, 2 * CH], F16, kind="Internal").ap()
    rc_scr = nc.dram_tensor("rc_scr", [NCH - 1, 128, 32], F32, kind="Internal").ap()
    w1t_scr = nc.dram_tensor("w1t_scr", [128, 8 * HF], F16, kind="Internal").ap()
    w2q_scr = nc.dram_tensor("w2q_scr", [128, 32 * 1024], FP8, kind="Internal").ap()
    winT_scr = nc.dram_tensor("winT_scr", [128, 8 * DPROJ], F16, kind="Internal").ap()
    woutT_scr = nc.dram_tensor("woutT_scr", [128, 16 * D], F16, kind="Internal").ap()
    cc_in = nc.dram_tensor("cc_in", [129, DSSM], F32, kind="Internal").ap()
    cc_out = nc.dram_tensor("cc_out", [NCORES * 129, DSSM], F32, kind="Internal").ap()

    ctx = ExitStack()
    with ctx:
        E = ctx.enter_context

        ar1 = E(nc.sbuf_tensor("ar1", [128, 8 * DPROJ], F16))
        arq = E(nc.sbuf_tensor("arq", [128, 32 * 1024], FP8))
        aq16 = arq[:].bitcast(F16)
        sm32 = E(nc.sbuf_tensor("sm32", [32, NCH * CH], F32))
        dt16a = E(nc.sbuf_tensor("dt16a", [128, NCH * CH], F16))
        bfc = E(nc.sbuf_tensor("bfc", [128, 2 * CH], F16))
        sm16 = E(nc.sbuf_tensor("sm16", [128, 2048], F16))
        smx = E(nc.sbuf_tensor("smx", [32, 32], F32))
        prun = E(nc.sbuf_tensor("prun", [1, DSSM], F32))
        c_id16 = E(nc.sbuf_tensor("c_id16", [128, 128], F16))
        c_id32 = E(nc.sbuf_tensor("c_id32", [32, 32], F32))
        c_ones = E(nc.sbuf_tensor("c_ones", [1, 128], F32))
        c_ones16 = E(nc.sbuf_tensor("c_ones16", [16, 128], F16))
        c_tril = E(nc.sbuf_tensor("c_tril", [128, 2 * CH], F16))
        c_eexp = E(nc.sbuf_tensor("c_eexp", [128, DSSM], F16))
        c_bdones = E(nc.sbuf_tensor("c_bdones", [128, 4 * CH], F16))
        c_segones = E(nc.sbuf_tensor("c_segones", [128, 128], F16))
        c_repl16 = E(nc.sbuf_tensor("c_repl16", [128, 128], F16))
        c_repl32 = E(nc.sbuf_tensor("c_repl32", [128, 128], F32))
        c_mcols = E(nc.sbuf_tensor("c_mcols", [128, NCORES], F32))
        c_imcols = E(nc.sbuf_tensor("c_imcols", [128, NCORES], F32))
        c_mnext = E(nc.sbuf_tensor("c_mnext", [128, 1], F32))
        c_imnext = E(nc.sbuf_tensor("c_imnext", [128, 1], F32))
        c_hmask = E(nc.sbuf_tensor("c_hmask", [128, 1], F32))
        c_b1 = E(nc.sbuf_tensor("c_b1", [128, HF // 128], F32))
        c_bin = E(nc.sbuf_tensor("c_bin", [128, NXT + 1], F32))
        c_convw = E(nc.sbuf_tensor("c_convw", [128, NXT * KCONV], F32))
        c_convb = E(nc.sbuf_tensor("c_convb", [128, NXT], F32))
        c_dtb = E(nc.sbuf_tensor("c_dtb", [32, 1], F32))
        c_alog = E(nc.sbuf_tensor("c_alog", [32, 1], F32))
        c_acol = E(nc.sbuf_tensor("c_acol", [32, 1], F32))
        c_dtcomb = E(nc.sbuf_tensor("c_dtcomb", [32, 1], F32))
        c_drow = E(nc.sbuf_tensor("c_drow", [1, NH], F32))
        c_normw = E(nc.sbuf_tensor("c_normw", [128, DSSM // 128], F32))
        rcld = E(nc.sbuf_tensor("rcld", [128, 32], F32))

        PF32 = E(nc.sbuf_tensor("PF32", [128, 9744], F32))
        PF16 = E(nc.sbuf_tensor("PF16", [128, 12288], F16))
        PQ8 = E(nc.sbuf_tensor("PQ8", [128, 8192], FP8))
        x1h = PF16[:, 10240:11264]
        xtile2 = PF32[:, 5648:9744]
        c_b2q = sm16[0:1, 0:1024]
        rowmm = PF16[0:1, 10240:12288]
        c_b2rep = PF32[:, 5136:5648].bitcast(F16)

        wstg = PF32[:, 0:4096]
        wq16 = PF16[:, 0:4096]
        tstg = PF16[:, 4096:6144]
        wq8 = PQ8[:, 0:4096]
        xtile = PF32[:, 0:4096]
        lnt = PF32[:, 4096:5120]
        lnc_ = PF32[:, 5120:5136]
        xn = PF16[:, 0:4096]
        xnf = PF16[:, 4096:8192]
        o16 = PF16[:, 8192:12288]
        lnsq16 = o16[:, 0:1024] if False else PF16[:, 8192:9216]
        xnh = PF16[:, 9216:10240]
        hq = PQ8[:, 0:8192]
        cacc = PF32[:, 0:512]
        x1c = PF32[:, 512:2560]
        lnt2 = PF32[:, 2560:3584]
        lnc2_ = PF32[:, 3584:3600]
        smtp = PF32[:, 3600:4112]
        wstf = PF32[:, 4112:4368]
        repcd = PF32[:, 4368:4400]
        sloc = PF32[:, 4400:6448]
        gath = PF32[:, 6448:8496]
        dmcur_row = PF32[0:1, 0:2048]
        x1c2 = PF32[:, 6448:8496]
        xbc16 = PF16[:, 0:NXT * CVW]
        xn2 = PF16[:, 4704:6752]
        xnf2 = PF16[:, 6752:8800]
        xsil = PF16[:, 8800:9312]
        btt = PF16[:, 9312:9568]
        lnsq16b = PF16[:, 8800:9824]
        x1c3 = PF32[:, 0:2048]
        sused = PF32[:, 2048:4096]
        carry = PF32[:, 4096:6144]
        slpf = PF32[:, 6144:8192]
        tmp3 = PF32[:, 8192:8704]
        att = PF32[:, 8704:9728]
        rsc3 = PF32[:, 9728:9744]
        yt = PF16[:, 0:4096]
        yf = PF16[:, 4096:6144]
        sused16 = PF16[:, 6144:8192]
        boutrep = PF16[:, 8192:9216]
        xw = aq16[:, 0:4096]
        szt = aq16[:, 4096:8192]
        xtt = aq16[:, 8192:12288]
        szex = aq16[:, 12288:16384]
        AR1U = 16 * D
        bdseg = ar1[0:8, AR1U:AR1U + 8192]
        bdlhs = ar1[0:8, AR1U + 8192:AR1U + 10240]
        mtt2 = [ar1[:, AR1U + 10240:AR1U + 12288], ar1[:, AR1U + 12288:AR1U + 14336]]
        di16 = ar1[:, AR1U + 14336:AR1U + 18432]
        gmm = PF16[:, 9216:9728]
        bdsegF = ar1[:, AR1U:AR1U + 8192]
        bdlhsF = ar1[:, AR1U + 8192:AR1U + 10240]
        rowmmF = PF16[:, 10240:12288]
        smtpF = PF32[:, 4080:4112]
        dmcurF = PF32[:, 0:2048]
        dtf16 = sm16[:, 0:CH]
        wstf16 = sm16[:, 256:512]
        ecsf16 = sm16[:, 512:768]
        cshl = sm16[:, 768:1280]
        ncshl = sm16[:, 1280:1792]
        cdcol = smx[:, 0:1]
        sumcs = smx[:, 1:2]
        pruncol = smx[:, 2:3]
        lnco = smx[:, 3:8]

        pseg = E(nc.psum_tensor("pseg", [128, 1024], F32))
        pmm0 = E(nc.psum_tensor("pmm0", [128, 512], F32))
        pmm1 = E(nc.psum_tensor("pmm1", [128, 512], F32))
        paux = E(nc.psum_tensor("paux", [128, 1024], F32))
        pt0 = E(nc.psum_tensor("pt0", [128, 1024], F16))
        pt1 = E(nc.psum_tensor("pt1", [128, 1024], F16))
        PT = [pt0, pt1]
        PMM = [pmm0, pmm1]

        b = Bld(nc, ctx)

        DT = {}

        def p0():
            def loads(e):
                lst = []
                for src, dst in [
                    (ident16, c_id16), (ident32, c_id32), (ones_row, c_ones),
                    (ones16, c_ones16), (trilT, c_tril), (eexp, c_eexp),
                    (bdones, c_bdones), (segones, c_segones),
                    (repl16, c_repl16), (repl32, c_repl32),
                    (mcols, c_mcols), (imcols, c_imcols),
                    (mnext, c_mnext), (imnext, c_imnext),
                    (halo_mask, c_hmask), (b1_col, c_b1), (bin_col, c_bin),
                    (convw_col, c_convw), (convb_col, c_convb),
                    (dtbias_col, c_dtb), (alog_col, c_alog), (d_row, c_drow),
                    (normw_col, c_normw),
                ]:
                    lst.append(e.dma_start(out=dst[:], in_=src))
                lst.append(e.dma_start(out=wstg[0:1, 0:D], in_=b2_row))
                return lst
            ld0 = b.dma("sync", loads, 24)
            mz0 = b.prod("vector", "v", lambda e: (
                e.memset(sm16[:], 0.0), e.memset(dt16a[:], 0.0))[-1], 2)
            b.w("scalar", ld0)
            sA0 = b.prod("scalar", "s", lambda e: e.activation(out=c_acol[:], in_=c_alog[:], func=AF.Exp))
            b.w("scalar", sA0)
            b.prod("scalar", "s", lambda e: e.activation(out=c_acol[:], in_=c_acol[:], func=AF.Copy, scale=-1.0))
            b.w("vector", ld0, mz0)
            vq = b.prod("vector", "v", lambda e: (
                e.tensor_copy(out=c_b2q[:], in_=wstg[0:1, 0:D]),
                e.tensor_tensor(out=c_dtcomb[:], in0=c_bin[0:32, NXT:NXT + 1], in1=c_dtb[:], op=OP.add))[-1], 2)
            b.w("vector", vq)
            vq2 = b.prod("vector", "v", lambda e: e.tensor_copy(out=wstg[0:1, 0:D], in_=c_b2q[:]))
            b.w("tensor", vq2)
            tD = b.prod("tensor", "t", lambda e: (
                e.matmul(paux[:, 0:512], c_ones[:], wstg[0:1, 0:512], start=True, stop=True),
                e.matmul(paux[:, 512:1024], c_ones[:], wstg[0:1, 512:1024], start=True, stop=True),
                e.matmul(pseg[:, 0:NH], c_ones[:], c_drow[:], start=True, stop=True))[-1], 3)
            b.w("vector", tD)
            vD = b.prod("vector", "v", lambda e: (
                e.tensor_copy(out=c_b2rep[:, 0:512], in_=paux[:, 0:512]),
                e.tensor_copy(out=c_b2rep[:, 512:1024], in_=paux[:, 512:1024]),
                e.tensor_copy(out=repcd[:, 0:NH], in_=pseg[:, 0:NH]))[-1], 3)
            b.w("scalar", vD)
            def dis(e):
                for h in range(NH):
                    ins = e.activation(out=di16[:, h * 128:(h + 1) * 128], in_=c_id16[:],
                                       func=AF.Copy, scale=repcd[:, h:h + 1])
                return ins
            b.prod("scalar", "s", dis)
            b.barrier()

            for q in range(4):
                b.dma("gpsimd", lambda e, q=q: e.dma_start(
                    out=ar1[:, q * 8192:(q + 1) * 8192],
                    in_=w1t_h[:, q * 8192:(q + 1) * 8192]))
            for q in range(4):
                b.dma("gpsimd", lambda e, q=q: e.dma_start(
                    out=arq[:, q * 8192:(q + 1) * 8192],
                    in_=w2t_h[:, q * 8192:(q + 1) * 8192]))

            ftok = [None, None]
            stok = [None, None]
            for r in range(8):
                sl = wstg[:, (r % 2) * 2048:(r % 2) * 2048 + 2048]
                b.w("sync", stok[r % 2])
                dwi = b.dma("sync", lambda e, r=r, sl=sl: e.dma_start(
                    out=sl, in_=woutt_h[:, r * 2048:(r + 1) * 2048]))
                b.w("vector", dwi, ftok[r % 2])
                ftok[r % 2] = b.prod("vector", "v", lambda e, r=r, sl=sl: [
                    e.tensor_scalar(out=sl[:, i * 1024:(i + 1) * 1024],
                                    in0=sl[:, i * 1024:(i + 1) * 1024],
                                    scalar1=c_normw[:, r * 2 + i:r * 2 + i + 1],
                                    scalar2=None, op0=OP.mult) for i in range(2)][-1], 2)
                b.w("gpsimd", ftok[r % 2])
                stok[r % 2] = b.dma("gpsimd", lambda e, r=r, sl=sl: e.dma_start(
                    out=woutT_scr[:, r * 2048:(r + 1) * 2048], in_=sl))
            b.barrier()
            DT["w1scr"] = b.dma("gpsimd", lambda e: e.dma_start(out=w1t_scr, in_=w1t_h[:]))
            DT["w2scr"] = b.dma("gpsimd", lambda e: e.dma_start(out=w2q_scr, in_=w2t_h[:]))
            DT["winscr"] = b.dma("gpsimd", lambda e: e.dma_start(out=winT_scr, in_=wint_h[:]))

        def layer_norm(src_ap, dst_ap, tb, sqd, cols):
            b.w("scalar", b.now("s"))
            b.w("vector", b.now("v"))
            v0 = b.prod("vector", "v", lambda e: e.tensor_reduce(
                out=cols[:, 0:1], in_=src_ap, axis=AX, op=OP.add, negate=True))
            b.w("vector", v0)
            v0b = b.prod("vector", "v", lambda e: e.tensor_scalar(
                out=cols[:, 0:1], in0=cols[:, 0:1], scalar1=1.0 / D, scalar2=None, op0=OP.mult))
            b.w("scalar", v0b)
            sm = b.prod("scalar", "s", lambda e: e.activation(
                out=tb, in_=src_ap, func=AF.Identity, bias=cols[:, 0:1]))
            b.w("scalar", sm)
            s0 = b.prod("scalar", "s", lambda e: e.activation(
                out=sqd, in_=tb, func=AF.Square, accum_out=cols[:, 1:2]))
            b.w("vector", s0)
            v1 = b.prod("vector", "v", lambda e: e.tensor_scalar(
                out=cols[:, 1:2], in0=cols[:, 1:2], scalar1=1.0 / D, scalar2=LN_EPS,
                op0=OP.mult, op1=OP.add))
            b.w("scalar", v1)
            s1 = b.prod("scalar", "s", lambda e: e.activation(
                out=cols[:, 2:3], in_=cols[:, 1:2], func=AF.Ln))
            b.w("scalar", s1)
            s1b = b.prod("scalar", "s", lambda e: e.activation(
                out=cols[:, 3:4], in_=cols[:, 2:3], func=AF.Exp, scale=-0.5))
            b.w("scalar", s1b)
            s2 = b.prod("scalar", "s", lambda e: e.activation(
                out=dst_ap, in_=tb, func=AF.Copy, scale=cols[:, 3:4]))
            return s2

        def ffn_phase(src_dram, res_dram, with_halo):
            ntile = T // 512
            XT = [xtile, xtile2]
            slot_tok = [None, None]
            evtok = [None, None]
            sev = [None, None]
            vev = [None, None]
            hq_rtok = None
            for it in range(ntile + (1 if with_halo else 0)):
                halo = with_halo and it == ntile
                nsub = 1 if halo else 4
                xt = XT[it % 2]
                if slot_tok[it % 2] is not None:
                    b.w("sync", *slot_tok[it % 2])
                def ldf(e, it=it, halo=halo, nsub=nsub, xt=xt):
                    return [e.dma_start(out=xt[:, st_ * D:(st_ + 1) * D],
                                        in_=(u_halo if halo else
                                             src_dram[it * 512 + st_ * 128: it * 512 + (st_ + 1) * 128, :]))
                            for st_ in range(nsub)]
                ldc = b.dma("sync", ldf, nsub)
                b.w("vector", ldc)
                b.w("scalar", ldc)
                for st_ in range(nsub):
                    layer_norm(xt[:, st_ * D:(st_ + 1) * D], xn[:, st_ * D:(st_ + 1) * D],
                               lnt, lnsq16, lnc_)
                lnS = b.now("s")
                b.w("tensor", lnS)
                for kt in range(8):
                    pt = PT[kt % 2]
                    b.w("tensor", evtok[kt % 2])
                    def tr(e, kt=kt, nsub=nsub, pt=pt):
                        for st_ in range(nsub):
                            ins = e.transpose(pt[:, st_ * 128:(st_ + 1) * 128],
                                              xn[:, st_ * D + kt * 128: st_ * D + (kt + 1) * 128],
                                              c_id16[:])
                        return ins
                    tt = b.prod("tensor", "t", tr)
                    b.w("vector", tt)
                    ncols = nsub * 128
                    evtok[kt % 2] = b.prod("vector", "v",
                        lambda e, kt=kt, pt=pt, ncols=ncols: e.tensor_copy(
                            out=xnf[:, kt * 512: kt * 512 + ncols], in_=pt[:, 0:ncols]))
                vtr = b.now("v")
                b.w("tensor", vtr)
                nhalf = 1 if halo else 2
                hcols = 128 if halo else 256
                for hv in range(nhalf):
                    b.w("scalar", hq_rtok)
                    for m in range(32):
                        pb = PMM[m % 2]
                        b.w("tensor", sev[m % 2], vev[m % 2])
                        def mm(e, m=m, pb=pb, hv=hv, hcols=hcols):
                            for kt in range(8):
                                ins = e.matmul(pb[:, 0:hcols],
                                               ar1[:, kt * HF + m * 128: kt * HF + (m + 1) * 128],
                                               xnf[:, kt * 512 + hv * 256: kt * 512 + hv * 256 + hcols],
                                               start=(kt == 0), stop=(kt == 7))
                            return ins
                        tmc = b.prod("tensor", "t", mm)
                        b.w("scalar", tmc)
                        sev[m % 2] = b.prod("scalar", "s",
                            lambda e, m=m, pb=pb, hcols=hcols: e.activation(
                                out=hq[:, m * 256: m * 256 + hcols], in_=pb[:, 0:hcols],
                                func=AF.Relu, bias=c_b1[:, m:m + 1]))
                    hqS = b.now("s")
                    b.w("tensor", hqS)
                    for mh in range(1 if halo else 2):
                        mt = hv * 2 + mh
                        for nd in range(2):
                            qq = (2 * mh + nd) % 2
                            pb = PMM[qq]
                            b.w("tensor", vev[qq], sev[qq])
                            def mm(e, mh=mh, nd=nd, pb=pb):
                                hqr = hq[:].rearrange("p (j c) -> p j c", c=256)
                                aqr = arq[:].rearrange("p (j c) -> p j c", c=1024)
                                for k in range(16):
                                    ins = e.matmul(pb[:, 0:512],
                                                   hqr[:, 2 * k:2 * k + 2, mh * 128:(mh + 1) * 128],
                                                   aqr[:, 2 * k:2 * k + 2, nd * 512:(nd + 1) * 512],
                                                   start=(k == 0), stop=(k == 15),
                                                   perf_mode=mybir.MatmulPerfMode.DoubleRow)
                                return ins
                            tmc = b.prod("tensor", "t", mm)
                            b.w("vector", tmc)
                            vev[qq] = b.prod("vector", "v",
                                lambda e, mt=mt, nd=nd, pb=pb: e.tensor_tensor(
                                    out=o16[:, mt * D + nd * 512: mt * D + (nd + 1) * 512],
                                    in0=pb[:, 0:512], in1=c_b2rep[:, nd * 512:(nd + 1) * 512],
                                    op=OP.add))
                    hq_rtok = b.now("t")
                b.w("vector", b.now("v"))
                def res(e, nsub=nsub, halo=halo, xt=xt):
                    for mt in range(nsub):
                        ins = e.scalar_tensor_tensor(
                            out=(x1h[:, 0:D] if halo else xt[:, mt * D:(mt + 1) * D]),
                            in0=o16[:, mt * D:(mt + 1) * D], scalar=0.5,
                            in1=xt[:, mt * D:(mt + 1) * D], op0=OP.mult, op1=OP.add)
                    return ins
                rV = b.prod("vector", "v", res)
                if halo:
                    b.w("scalar", rV)
                    b.w("vector", rV)
                    layer_norm(x1h[:, 0:D], xnh[:], lnt, lnsq16, lnc_)
                else:
                    b.w("gpsimd", rV)
                    def stt(e, it=it, xt=xt):
                        return [e.dma_start(out=res_dram[it * 512 + mt * 128:
                                                         it * 512 + (mt + 1) * 128, :],
                                            in_=xt[:, mt * D:(mt + 1) * D]) for mt in range(4)]
                    sdc = b.dma("gpsimd", stt, 4)
                    slot_tok[it % 2] = (lnS, rV, sdc)
            b.barrier()

        def p2():
            b.w("sync", DT["winscr"])
            b.dma("sync", lambda e: e.dma_start(out=ar1[:, 0:8 * DPROJ], in_=winT_scr))
            b.dma("sync", lambda e: e.dma_start(out=rowmm, in_=bz_row))
            b.barrier()
            evtok = [None, None]
            for kt in range(8):
                pt = PT[kt % 2]
                b.w("tensor", evtok[kt % 2])
                tt = b.prod("tensor", "t", lambda e, kt=kt, pt=pt: e.transpose(
                    pt[:, 0:128], xnh[:, kt * 128:(kt + 1) * 128], c_id16[:]))
                b.w("vector", tt)
                evtok[kt % 2] = b.prod("vector", "v", lambda e, kt=kt, pt=pt: e.tensor_copy(
                    out=xnf2[:, kt * 256: kt * 256 + 128], in_=pt[:, 0:128]))
            b.w("tensor", b.now("v"))
            sev = [None, None]
            for mf in range(NXT):
                pb = PMM[mf % 2]
                b.w("tensor", sev[mf % 2])
                def mm(e, mf=mf, pb=pb):
                    for kt in range(8):
                        ins = e.matmul(pb[:, 0:128],
                                       ar1[:, kt * DPROJ + DSSM + mf * 128:
                                           kt * DPROJ + DSSM + (mf + 1) * 128],
                                       xnf2[:, kt * 256: kt * 256 + 128],
                                       start=(kt == 0), stop=(kt == 7))
                    return ins
                tmc = b.prod("tensor", "t", mm)
                b.w("scalar", tmc)
                sev[mf % 2] = b.prod("scalar", "s", lambda e, mf=mf, pb=pb: e.activation(
                    out=cacc[:, (mf % 2) * CH: (mf % 2) * CH + 3], in_=pb[:, 125:128],
                    func=AF.Identity, bias=c_bin[:, mf:mf + 1]))
                b.w("vector", sev[mf % 2])
                vv = b.prod("vector", "v", lambda e, mf=mf: e.tensor_scalar(
                    out=xbc16[:, mf * CVW: mf * CVW + 3],
                    in0=cacc[:, (mf % 2) * CH: (mf % 2) * CH + 3],
                    scalar1=c_hmask[:, 0:1], scalar2=None, op0=OP.mult))
                b.w("scalar", vv)
            b.prod("vector", "v", lambda e: (e.memset(sloc[:], 0.0),
                                             e.memset(smx[:, 0:8], 0.0),
                                             e.memset(smtpF[:], 0.0))[-1], 1)
            b.barrier()

            XC = [x1c, x1c2]
            slot_tok = [None, None]
            evtok = [None, None]
            sev = [None, None]
            sctok = [None, None]
            xwtok = [None, None]
            siltok = [None, None]
            xtrtok = [None, None]
            dgtok = [None, None]
            cvtok = [None, None]
            pseg_rd = None
            cacc16 = cacc[:].bitcast(F16)
            xnf_rt = None
            xn_rt = None
            szt_sp = None
            xtt_sp = None
            bfc_sp = None
            sl_sp = None
            rc_sp = None
            halv = None
            for c in range(NCH):
                csfc = sm32[:, c * CH:(c + 1) * CH]
                dt16c = dt16a[0:32, c * CH:(c + 1) * CH]
                xc = XC[c % 2]
                if slot_tok[c % 2] is not None:
                    b.w("sync", slot_tok[c % 2])
                ldc = b.dma("sync", lambda e, c=c, xc=xc: [
                    e.dma_start(out=xc[:, 0:D], in_=x1_scr[c * CH: c * CH + 128, :]),
                    e.dma_start(out=xc[:, D:2 * D], in_=x1_scr[c * CH + 128:(c + 1) * CH, :])], 2)
                b.w("vector", ldc)
                b.w("scalar", ldc, xn_rt)
                for th in range(2):
                    layer_norm(xc[:, th * D:(th + 1) * D], xn2[:, th * D:(th + 1) * D],
                               lnt2, lnsq16b, lnc2_)
                lnS = b.now("s")
                slot_tok[c % 2] = lnS
                b.w("tensor", lnS)
                for kt in range(8):
                    pt = PT[kt % 2]
                    b.w("tensor", evtok[kt % 2])
                    def tr(e, kt=kt, pt=pt):
                        for th in range(2):
                            ins = e.transpose(pt[:, th * 128:(th + 1) * 128],
                                              xn2[:, th * D + kt * 128: th * D + (kt + 1) * 128],
                                              c_id16[:])
                        return ins
                    tt = b.prod("tensor", "t", tr)
                    b.w("vector", tt, xnf_rt)
                    evtok[kt % 2] = b.prod("vector", "v", lambda e, kt=kt, pt=pt: e.tensor_copy(
                        out=xnf2[:, kt * 256:(kt + 1) * 256], in_=pt[:, 0:256]))
                xn_rt = b.now("t")
                b.w("tensor", b.now("v"))

                b.w("scalar", szt_sp)
                for mt in range(2):
                    for nz in range(4):
                        qq = (mt * 4 + nz) % 2
                        pb = PMM[qq]
                        b.w("tensor", sev[qq], sctok[qq])
                        def mm(e, mt=mt, nz=nz, pb=pb):
                            for kt in range(8):
                                e.matmul(pb[:, 0:512],
                                         xnf2[:, kt * 256 + mt * 128: kt * 256 + (mt + 1) * 128],
                                         ar1[:, kt * DPROJ + nz * 512: kt * DPROJ + (nz + 1) * 512],
                                         start=(kt == 0), stop=False)
                            return e.matmul(pb[:, 0:512], c_repl16[:],
                                            rowmmF[:, nz * 512:(nz + 1) * 512],
                                            start=False, stop=True)
                        tmc = b.prod("tensor", "t", mm)
                        b.w("scalar", tmc)
                        sev[qq] = b.prod("scalar", "s", lambda e, mt=mt, nz=nz, pb=pb: e.activation(
                            out=szt[:, mt * DSSM + nz * 512: mt * DSSM + (nz + 1) * 512],
                            in_=pb[:, 0:512], func=AF.Silu))
                b.w("gpsimd", b.now("s"))
                szt_sp = b.dma("gpsimd", lambda e, c=c: e.dma_start(out=sz_scr[c], in_=szt[:]))

                b.w("scalar", halv)
                for mf in range(NXT):
                    pb = PMM[mf % 2]
                    b.w("tensor", sev[mf % 2])
                    def mm(e, mf=mf, pb=pb):
                        for kt in range(8):
                            ins = e.matmul(pb[:, 0:CH],
                                           ar1[:, kt * DPROJ + DSSM + mf * 128:
                                               kt * DPROJ + DSSM + (mf + 1) * 128],
                                           xnf2[:, kt * 256:(kt + 1) * 256],
                                           start=(kt == 0), stop=(kt == 7))
                        return ins
                    tmc = b.prod("tensor", "t", mm)
                    b.w("scalar", tmc)
                    sev[mf % 2] = b.prod("scalar", "s", lambda e, mf=mf, pb=pb: e.activation(
                        out=xbc16[:, mf * CVW + 3: mf * CVW + 3 + CH], in_=pb[:, 0:CH],
                        func=AF.Identity, bias=c_bin[:, mf:mf + 1]))
                pb = PMM[0]
                b.w("tensor", sev[0])
                tmc = b.prod("tensor", "t", lambda e, pb=pb: [
                    e.matmul(pb[0:32, 0:CH],
                             ar1[:, kt * DPROJ + DSSM + NXT * 128: kt * DPROJ + DSSM + NXT * 128 + 32],
                             xnf2[:, kt * 256:(kt + 1) * 256],
                             start=(kt == 0), stop=(kt == 7)) for kt in range(8)][-1])
                xnf_rt = b.now("t")
                tmc_dt = tmc

                b.w("tensor", b.now("s"), halv)
                for mf in range(NXT):
                    qq = mf % 2
                    dsl = cacc16[:, qq * 512: qq * 512 + 512]
                    pcv = pseg[:, 0:CH] if qq == 0 else pseg[:, 512:512 + CH]
                    b.w("gpsimd", cvtok[qq])
                    def dg(e, mf=mf, dsl=dsl):
                        for k in range(KCONV):
                            ins = e.tensor_tensor(
                                out=dsl[:, k * 128:(k + 1) * 128], in0=c_id16[:],
                                in1=c_convw[:, mf * KCONV + k: mf * KCONV + k + 1]
                                    .broadcast_to([128, 128]),
                                op=OP.mult)
                        return ins
                    dgtok[qq] = b.prod("gpsimd", "g", dg, KCONV)
                    b.w("tensor", dgtok[qq], siltok[qq])
                    if qq == 0:
                        b.w("tensor", pseg_rd)
                    def cmm(e, mf=mf, dsl=dsl, pcv=pcv):
                        for k in range(KCONV):
                            ins = e.matmul(pcv, dsl[:, k * 128:(k + 1) * 128],
                                           xbc16[:, mf * CVW + k: mf * CVW + k + CH],
                                           start=(k == 0), stop=(k == KCONV - 1))
                        return ins
                    cvtok[qq] = b.prod("tensor", "t", cmm)
                    b.w("scalar", cvtok[qq], xtrtok[qq])
                    if mf < 16:
                        b.w("scalar", xtt_sp)
                        siltok[qq] = b.prod("scalar", "s", lambda e, qq=qq, mf=mf, pcv=pcv: e.activation(
                            out=xsil[:, qq * CH: qq * CH + CH], in_=pcv,
                            func=AF.Silu, bias=c_convb[:, mf:mf + 1]))
                        b.w("tensor", siltok[qq], evtok[qq])
                        pt = PT[qq]
                        def tr(e, qq=qq, pt=pt):
                            for th in range(2):
                                ins = e.transpose(pt[:, th * 128:(th + 1) * 128],
                                                  xsil[:, qq * CH + th * 128: qq * CH + (th + 1) * 128],
                                                  c_id16[:])
                            return ins
                        tt = b.prod("tensor", "t", tr)
                        xtrtok[qq] = tt
                        b.w("vector", tt)
                        evtok[qq] = b.prod("vector", "v", lambda e, mf=mf, pt=pt: (
                            e.tensor_copy(out=xtt[:, mf * 128:(mf + 1) * 128], in_=pt[:, 0:128]),
                            e.tensor_copy(out=xtt[:, DSSM + mf * 128: DSSM + (mf + 1) * 128],
                                          in_=pt[:, 128:256]))[-1], 2)
                    else:
                        b.w("scalar", bfc_sp)
                        siltok[qq] = b.prod("scalar", "s", lambda e, mf=mf, pcv=pcv: e.activation(
                            out=bfc[:, (mf - 16) * CH: (mf - 15) * CH], in_=pcv,
                            func=AF.Silu, bias=c_convb[:, mf:mf + 1]))
                b.w("gpsimd", b.now("v"), b.now("s"))
                xtt_sp = b.dma("gpsimd", lambda e, c=c: e.dma_start(out=xt_scr[c], in_=xtt[:]))
                bfc_sp = b.dma("gpsimd", lambda e, c=c: e.dma_start(out=bc_scr[c], in_=bfc[:]))
                b.w("scalar", tmc_dt)
                sdt0 = b.prod("scalar", "s", lambda e, pb=PMM[0]: e.activation(
                    out=smtp[0:32, 0:CH], in_=pb[0:32, 0:CH], func=AF.Exp, bias=c_dtcomb[:, 0:1]))
                sev[0] = sdt0
                b.w("scalar", sdt0)
                sdt = b.prod("scalar", "s", lambda e: e.activation(
                    out=smtp[0:32, CH:2 * CH], in_=smtp[0:32, 0:CH], func=AF.Ln, bias=1.0))
                b.w("vector", sdt)
                vdt16 = b.prod("vector", "v", lambda e, dt16c=dt16c: e.tensor_copy(
                    out=dt16c, in_=smtp[0:32, CH:2 * CH]))
                b.w("vector", b.now("s"), b.now("v"), b.now("t"))
                halv = b.prod("vector", "v", lambda e: e.tensor_copy(
                    out=xbc16[:].rearrange("p (m w) -> p m w", w=CVW)[:, :, 0:3],
                    in_=xbc16[:].rearrange("p (m w) -> p m w", w=CVW)[:, :, CH:CH + 3]))

                vda = b.prod("vector", "v", lambda e: e.tensor_scalar(
                    out=smtp[0:32, 0:CH], in0=smtp[0:32, CH:2 * CH],
                    scalar1=c_acol[:, 0:1], scalar2=None, op0=OP.mult))
                cur, oth = 0, 1
                for kbit in range(8):
                    s_ = 1 << kbit
                    b.w("vector", b.now("v"))
                    b.prod("vector", "v", lambda e, s_=s_, cur=cur, oth=oth: (
                        e.tensor_copy(out=smtp[0:32, oth * CH: oth * CH + s_],
                                      in_=smtp[0:32, cur * CH: cur * CH + s_]),
                        e.tensor_tensor(out=smtp[0:32, oth * CH + s_: oth * CH + CH],
                                        in0=smtp[0:32, cur * CH + s_: cur * CH + CH],
                                        in1=smtp[0:32, cur * CH: cur * CH + CH - s_],
                                        op=OP.add))[-1], 2)
                    cur, oth = oth, cur
                b.w("vector", b.now("v"))
                vcs = b.prod("vector", "v", lambda e, csfc=csfc, cur=cur: e.tensor_copy(
                    out=csfc, in_=smtp[0:32, cur * CH: cur * CH + CH]))
                b.w("scalar", vcs)
                scd = b.prod("scalar", "s", lambda e, csfc=csfc: (
                    e.activation(out=cdcol, in_=csfc[:, CH - 1:CH], func=AF.Exp),
                    e.activation(out=wstf[0:32, 0:CH], in_=csfc, func=AF.Exp,
                                 scale=-1.0, bias=csfc[:, CH - 1:CH]))[-1], 2)
                b.w("vector", scd)
                vws0 = b.prod("vector", "v", lambda e, dt16c=dt16c: e.tensor_tensor(
                    out=wstf[0:32, 0:CH], in0=wstf[0:32, 0:CH], in1=dt16c, op=OP.mult))
                b.w("vector", vws0)
                vws = b.prod("vector", "v", lambda e: e.tensor_copy(
                    out=wstf16[0:32, :], in_=wstf[0:32, 0:CH]))
                b.w("tensor", scd, b.now("v"))
                tcd = b.prod("tensor", "t", lambda e: e.transpose(
                    pseg[0:1, 0:32], cdcol, c_id32[:]))
                b.w("vector", tcd)
                vc1 = b.prod("vector", "v", lambda e: e.tensor_copy(
                    out=smtp[0:1, 480:512], in_=pseg[0:1, 0:32]))
                pseg_rd = vc1
                b.w("tensor", vc1)
                trc = b.prod("tensor", "t", lambda e: e.matmul(
                    paux[:, 0:32], c_repl32[:], smtpF[:], start=True, stop=True))
                b.w("vector", trc, rc_sp)
                vrc = b.prod("vector", "v", lambda e: e.tensor_copy(out=repcd[:, 0:32], in_=paux[:, 0:32]))
                if c < NCH - 1:
                    b.w("gpsimd", vrc)
                    rc_sp = b.dma("gpsimd", lambda e, c=c: e.dma_start(
                        out=rc_scr[c], in_=repcd[:, 0:32]), key="x0")

                pt = PT[0]
                b.w("tensor", b.now("v"))
                tbt = b.prod("tensor", "t", lambda e, pt=pt: (
                    e.transpose(pt[:, 0:128], bfc[:, 0:128], c_id16[:]),
                    e.transpose(pt[:, 128:256], bfc[:, 128:256], c_id16[:]))[-1], 2)
                evtok[0] = None
                b.w("vector", tbt)
                vbt = b.prod("vector", "v", lambda e, pt=pt: e.tensor_copy(
                    out=btt[:], in_=pt[:, 0:256]))
                evtok[0] = vbt

                b.w("tensor", vws, vbt, b.now("v"))
                for lh in range(2):
                    for sl in range(4):
                        qq = sl % 2
                        slot = paux[:, qq * 512: qq * 512 + 512]
                        b.w("tensor", xwtok[qq])
                        tws = b.prod("tensor", "t", lambda e, lh=lh, sl=sl, slot=slot: e.matmul(
                            slot, wstf16[:, lh * 128:(lh + 1) * 128],
                            c_eexp[:, sl * 512:(sl + 1) * 512], start=True, stop=True))
                        b.w("vector", tws)
                        xwtok[qq] = b.prod("vector", "v",
                            lambda e, lh=lh, sl=sl, slot=slot: e.tensor_tensor(
                                out=xw[:, lh * DSSM + sl * 512: lh * DSSM + (sl + 1) * 512],
                                in0=xtt[:, lh * DSSM + sl * 512: lh * DSSM + (sl + 1) * 512],
                                in1=slot, op=OP.mult))
                b.w("tensor", b.now("v"), b.now("s"))
                b.w("vector", sl_sp)
                for sl in range(4):
                    qq = sl % 2
                    pb = PMM[qq]
                    b.w("tensor", sctok[qq])
                    def mst(e, sl=sl, pb=pb):
                        for lh in range(2):
                            ins = e.matmul(pb[:, 0:512], btt[:, lh * 128:(lh + 1) * 128],
                                           xw[:, lh * DSSM + sl * 512: lh * DSSM + (sl + 1) * 512],
                                           start=(lh == 0), stop=(lh == 1))
                        return ins
                    tst = b.prod("tensor", "t", mst)
                    b.w("vector", tst)
                    def scan(e, sl=sl, pb=pb):
                        for hh in range(8):
                            h = sl * 8 + hh
                            ins = e.scalar_tensor_tensor(
                                out=sloc[:, h * HD:(h + 1) * HD],
                                in0=sloc[:, h * HD:(h + 1) * HD],
                                scalar=repcd[:, h:h + 1], in1=pb[:, hh * HD:(hh + 1) * HD],
                                op0=OP.mult, op1=OP.add)
                        return ins
                    sctok[qq] = b.prod("vector", "v", scan, 8)
                if c < NCH - 1:
                    b.w("gpsimd", b.now("v"))
                    sl_sp = b.dma("gpsimd", lambda e, c=c: e.dma_start(out=sl_scr[c], in_=sloc[:]))
            b.barrier()

        def px():
            v0 = b.prod("vector", "v", lambda e: e.tensor_reduce(
                out=sumcs, in_=sm32[:].rearrange("p (c l) -> p l c", l=CH)[:, CH - 1:CH, :],
                axis=AX, op=OP.add))
            b.w("scalar", v0)
            s0 = b.prod("scalar", "s", lambda e: e.activation(
                out=pruncol, in_=sumcs, func=AF.Exp))
            b.w("tensor", s0)
            t0 = b.prod("tensor", "t", lambda e: e.transpose(
                pseg[0:1, 0:32], pruncol, c_id32[:]))
            b.w("vector", t0)
            v1 = b.prod("vector", "v", lambda e: e.tensor_copy(
                out=prun[:].rearrange("a (h p) -> a h p", p=HD),
                in_=pseg[0:1, 0:32].unsqueeze(2).broadcast_to([1, 32, HD])))
            b.w("sync", v1)
            d1 = b.dma("sync", lambda e: e.dma_start(out=cc_in[0:128, :], in_=sloc[:]))
            d2 = b.dma("sync", lambda e: e.dma_start(out=cc_in[128:129, :], in_=prun[:]))
            b.w("gpsimd", d1, d2)
            cc = b.prod("gpsimd", "g", lambda e: e.collective_compute(
                "AllGather", OP.bypass, ins=[cc_in], outs=[cc_out],
                replica_groups=[list(range(NCORES))]))
            b.w("sync", cc)
            def ldr(e):
                return [e.dma_start(out=gath[j:j + 1, 0:DSSM],
                                    in_=cc_out[(j + 1) * 129 + 128: (j + 1) * 129 + 129, :])
                        for j in range(7)]
            d3 = b.dma("sync", ldr, 7)
            b.w("vector", d3)
            vdm = b.prod("vector", "v", lambda e: e.tensor_scalar(
                out=gath[0:7, 0:DSSM], in0=gath[0:7, 0:DSSM],
                scalar1=c_mnext[0:7, 0:1], scalar2=c_imnext[0:7, 0:1],
                op0=OP.mult, op1=OP.add))
            vz = b.prod("vector", "v", lambda e: e.memset(carry[:], 0.0))
            prevj = vz
            for j in range(7):
                b.w("sync", vdm)
                dmj = b.dma("sync", lambda e, j=j: e.dma_start(
                    out=dmcur_row, in_=gath[j:j + 1, 0:DSSM]))
                dj = b.dma("sync", lambda e, j=j: e.dma_start(
                    out=sused[:], in_=cc_out[j * 129: j * 129 + 128, :]))
                b.w("vector", dj)
                for slpair in range(2):
                    b.w("tensor", dmj, prevj)
                    def rep2(e, slpair=slpair):
                        e.matmul(paux[:, 0:512], c_repl32[:],
                                 dmcurF[:, slpair * 1024: slpair * 1024 + 512],
                                 start=True, stop=True)
                        return e.matmul(paux[:, 512:1024], c_repl32[:],
                                        dmcurF[:, slpair * 1024 + 512:(slpair + 1) * 1024],
                                        start=True, stop=True)
                    trp = b.prod("tensor", "t", rep2)
                    b.w("vector", trp)
                    cb1 = b.prod("vector", "v", lambda e, j=j, slpair=slpair: [
                        e.scalar_tensor_tensor(
                            out=carry[:, (slpair * 2 + qq) * 512:(slpair * 2 + qq + 1) * 512],
                            in0=sused[:, (slpair * 2 + qq) * 512:(slpair * 2 + qq + 1) * 512],
                            scalar=c_mcols[:, j:j + 1],
                            in1=carry[:, (slpair * 2 + qq) * 512:(slpair * 2 + qq + 1) * 512],
                            op0=OP.mult, op1=OP.add) for qq in range(2)][-1], 2)
                    b.w("vector", cb1)
                    prevj = b.prod("vector", "v", lambda e, slpair=slpair: [
                        e.tensor_tensor(
                            out=carry[:, (slpair * 2 + qq) * 512:(slpair * 2 + qq + 1) * 512],
                            in0=carry[:, (slpair * 2 + qq) * 512:(slpair * 2 + qq + 1) * 512],
                            in1=paux[:, qq * 512:(qq + 1) * 512], op=OP.mult) for qq in range(2)][-1], 2)
                b.w("sync", prevj)
            b.prod("vector", "v", lambda e: e.memset(smx[:, 1:2], 0.0))
            b.barrier()

        def p3():
            b.dma("sync", lambda e: e.dma_start(out=ar1[:, 0:16 * D], in_=woutT_scr))
            dbo = b.dma("sync", lambda e: e.dma_start(out=rowmm[0:1, 0:D], in_=bout_row))
            b.w("tensor", dbo)
            tbo = b.prod("tensor", "t", lambda e: (
                e.matmul(paux[:, 0:512], c_repl16[:], rowmmF[:, 0:512], start=True, stop=True),
                e.matmul(paux[:, 512:1024], c_repl16[:], rowmmF[:, 512:1024], start=True, stop=True))[-1], 1)
            b.w("vector", tbo)
            b.prod("vector", "v", lambda e: (
                e.tensor_copy(out=boutrep[:, 0:512], in_=paux[:, 0:512]),
                e.tensor_copy(out=boutrep[:, 512:1024], in_=paux[:, 512:1024]))[-1], 1)
            b.prod("vector", "v", lambda e: (e.memset(bdseg, 0.0),
                                             e.memset(bdlhs, 0.0))[-1], 1)
            b.prod("tensor", "t", lambda e: e.matmul(
                pseg[:, 0:NH], c_ones[:], c_drow[:], start=True, stop=True))
            b.w("vector", b.now("t"))
            b.prod("vector", "v", lambda e: e.tensor_copy(out=tmp3[:, 0:NH], in_=pseg[:, 0:NH]))
            b.w("scalar", b.now("v"))
            def dis(e):
                for h in range(NH):
                    ins = e.activation(out=di16[:, h * 128:(h + 1) * 128], in_=c_id16[:],
                                       func=AF.Copy, scale=tmp3[:, h:h + 1])
                return ins
            b.prod("scalar", "s", dis)
            b.barrier()

            prev_t = None
            prev_v = None
            x2_sp = None
            dsc = None
            for c in range(NCH):
                csfc = sm32[:, c * CH:(c + 1) * CH]
                dt16c = dt16a[0:32, c * CH:(c + 1) * CH]
                dt16cF = dt16a[:, c * CH:(c + 1) * CH]
                b.w("sync", prev_t, prev_v, x2_sp)
                ldc = b.dma("sync", lambda e, c=c: [
                    e.dma_start(out=xtt[:], in_=xt_scr[c]),
                    e.dma_start(out=szt[:], in_=sz_scr[c]),
                    e.dma_start(out=bfc[:], in_=bc_scr[c]),
                    e.dma_start(out=x1c3[:, 0:D], in_=x1_scr[c * CH: c * CH + 128, :]),
                    e.dma_start(out=x1c3[:, D:2 * D], in_=x1_scr[c * CH + 128:(c + 1) * CH, :])], 5)
                ldp = None
                if c > 0:
                    ldp = b.dma("sync", lambda e, c=c: [
                        e.dma_start(out=slpf[:, q * 512:(q + 1) * 512],
                                    in_=sl_scr[c - 1].rearrange("p (q f) -> p q f", q=4)
                                    [:, q, :]) for q in range(4)], 4)
                    ldr = b.dma("sync", lambda e, c=c: e.dma_start(out=rcld[:], in_=rc_scr[c - 1]), key="x1")
                if c == 0:
                    b.w("vector", b.now("t"))
                    b.prod("vector", "v", lambda e: e.tensor_copy(out=sused[:], in_=carry[:]))
                else:
                    b.w("vector", ldr, b.now("v"))
                    vcar = b.prod("vector", "v", lambda e: e.tensor_tensor(
                        out=carry[:].rearrange("p (h d) -> p h d", d=HD),
                        in0=carry[:].rearrange("p (h d) -> p h d", d=HD),
                        in1=rcld[:].unsqueeze(2).broadcast_to([128, 32, HD]),
                        op=OP.mult))
                    b.w("vector", ldp, vcar)
                    b.prod("vector", "v", lambda e: [
                        e.tensor_tensor(out=sused[:, hh * 1024:(hh + 1) * 1024],
                                        in0=carry[:, hh * 1024:(hh + 1) * 1024],
                                        in1=slpf[:, hh * 1024:(hh + 1) * 1024],
                                        op=OP.add) for hh in range(2)][-1], 2)

                b.w("vector", b.now("v"))
                vs16 = b.prod("vector", "v", lambda e: e.tensor_copy(out=sused16[:], in_=sused[:]))
                b.w("tensor", ldc, b.now("v"))
                tg = b.prod("tensor", "t", lambda e: [
                    e.matmul(pseg[:, jh * 512: jh * 512 + CH],
                             bfc[:, jh * 128:(jh + 1) * 128],
                             bfc[:, 256:512],
                             start=True, stop=True) for jh in range(2)][-1])
                b.w("vector", tg)
                vg = b.prod("vector", "v", lambda e: (
                    e.tensor_tensor(out=gmm[:, 0:CH], in0=pseg[:, 0:CH],
                                    in1=c_tril[:, 0:CH], op=OP.mult),
                    e.tensor_tensor(out=gmm[:, CH:2 * CH], in0=pseg[:, 512:512 + CH],
                                    in1=c_tril[:, CH:2 * CH], op=OP.mult))[-1], 2)

                vhi = b.prod("vector", "v", lambda e, csfc=csfc: e.tensor_copy(
                    out=cshl[0:32, 0:CH], in_=csfc))
                b.w("vector", vhi)
                vlo = b.prod("vector", "v", lambda e, csfc=csfc: e.tensor_tensor(
                    out=smtp[0:32, 0:CH], in0=csfc, in1=cshl[0:32, 0:CH], op=OP.subtract))
                b.w("vector", vlo)
                vhl = b.prod("vector", "v", lambda e: (
                    e.tensor_copy(out=cshl[0:32, CH:2 * CH], in_=smtp[0:32, 0:CH]),
                    e.tensor_scalar(out=ncshl[0:32, 0:CH], in0=cshl[0:32, 0:CH], scalar1=-1.0,
                                    scalar2=None, op0=OP.mult),
                    e.tensor_scalar(out=ncshl[0:32, CH:2 * CH], in0=smtp[0:32, 0:CH], scalar1=-1.0,
                                    scalar2=None, op0=OP.mult))[-1], 3)
                sec = b.prod("scalar", "s", lambda e, csfc=csfc: e.activation(
                    out=ecsf16[0:32, :], in_=csfc, func=AF.Exp))
                b.w("gpsimd", vhl, sec, b.now("t"), dsc if c > 0 else None)
                def scatw(e):
                    return [e.dma_start(out=bd_scr[0:32, :], in_=cshl[0:32, :]),
                            e.dma_start(out=bd_scr[32:64, :], in_=ncshl[0:32, :])]
                dscw = b.dma("gpsimd", scatw, 2)
                b.w("gpsimd", dscw)
                def scatr(e):
                    lst = []
                    for l in range(2):
                        for p in range(4):
                            lst.append(e.dma_start(
                                out=bdseg[l * 4 + p: l * 4 + p + 1, :]
                                    .rearrange("a (g c2) -> a g c2", g=8)[:, :, p * CH:(p + 1) * CH],
                                in_=bd_scr[0:32, :].rearrange("(g p) (l c2) -> p g l c2", p=4, l=2)
                                    [p:p + 1, :, l:l + 1, :].squeeze(2)))
                        lst.append(e.dma_start(
                            out=bdlhs[l * 4:(l + 1) * 4, :].rearrange("p (g j c2) -> p g j c2", g=8, j=2),
                            in_=bd_scr[32:64, :].rearrange("(g p) (l j c2) -> p g l j c2", p=4, l=2, j=2)
                                [:, :, l:l + 1, :, :].squeeze(2)))
                    return lst
                dsc = b.dma("gpsimd", scatr, 10)

                b.w("tensor", b.now("v"), sec)
                xwtok = [None, None]
                for lh in range(2):
                    for sl in range(4):
                        qq = sl % 2
                        slot = paux[:, qq * 512: qq * 512 + 512]
                        b.w("tensor", xwtok[qq])
                        tws = b.prod("tensor", "t", lambda e, lh=lh, sl=sl, slot=slot, dt16cF=dt16cF: e.matmul(
                            slot, dt16cF[:, lh * 128:(lh + 1) * 128],
                            c_eexp[:, sl * 512:(sl + 1) * 512], start=True, stop=True))
                        b.w("vector", tws)
                        xwtok[qq] = b.prod("vector", "v",
                            lambda e, lh=lh, sl=sl, slot=slot: e.tensor_tensor(
                                out=xw[:, lh * DSSM + sl * 512: lh * DSSM + (sl + 1) * 512],
                                in0=xtt[:, lh * DSSM + sl * 512: lh * DSSM + (sl + 1) * 512],
                                in1=slot, op=OP.mult))
                for ih in range(2):
                    for sl in range(4):
                        qq = sl % 2
                        slot = paux[:, qq * 512: qq * 512 + 512]
                        b.w("tensor", xwtok[qq])
                        tws = b.prod("tensor", "t", lambda e, ih=ih, sl=sl, slot=slot: e.matmul(
                            slot, ecsf16[:, ih * 128:(ih + 1) * 128],
                            c_eexp[:, sl * 512:(sl + 1) * 512], start=True, stop=True))
                        b.w("vector", tws)
                        xwtok[qq] = b.prod("vector", "v",
                            lambda e, ih=ih, sl=sl, slot=slot: e.tensor_tensor(
                                out=szex[:, ih * DSSM + sl * 512: ih * DSSM + (sl + 1) * 512],
                                in0=szt[:, ih * DSSM + sl * 512: ih * DSSM + (sl + 1) * 512],
                                in1=slot, op=OP.mult))

                b.w("tensor", dsc, b.now("v"))
                ytok = None
                mrd = [None, None]
                segrd = None
                gdone = {}
                PSEG2 = [pseg, paux]
                for g in range(9):
                    if g < 8:
                        mslot = mtt2[g % 2]
                        for jh in range(2):
                            ps = PSEG2[jh]
                            b.w("tensor", mrd[g % 2], segrd)
                            def sgm(e, g=g, jh=jh, ps=ps):
                                for nd in range(2):
                                    e.matmul(ps[:, nd * 512:(nd + 1) * 512], c_segones[:],
                                             bdsegF[:, g * 1024 + nd * 512: g * 1024 + (nd + 1) * 512],
                                             start=True, stop=False)
                                    ins = e.matmul(ps[:, nd * 512:(nd + 1) * 512],
                                                   bdlhsF[:, (g * 2 + jh) * 128:(g * 2 + jh + 1) * 128],
                                                   c_bdones[:, nd * 512:(nd + 1) * 512],
                                                   start=False, stop=True)
                                return ins
                            tsg = b.prod("tensor", "t", sgm)
                            b.w("scalar", tsg, mrd[g % 2])
                            sxp = b.prod("scalar", "s", lambda e, ps=ps, mslot=mslot, jh=jh: e.activation(
                                out=mslot[:, jh * 1024:(jh + 1) * 1024],
                                in_=ps[:, 0:1024], func=AF.Exp))
                            segrd = sxp
                            b.w("vector", sxp, vg)
                            mtok = b.prod("vector", "v", lambda e, mslot=mslot, jh=jh: e.scalar_tensor_tensor(
                                out=mslot[:, jh * 1024:(jh + 1) * 1024].rearrange("p (r c2) -> p r c2", r=4),
                                scalar=1.0,
                                in0=mslot[:, jh * 1024:(jh + 1) * 1024].rearrange("p (r c2) -> p r c2", r=4),
                                in1=gmm[:, jh * CH:(jh + 1) * CH].unsqueeze(1).broadcast_to([128, 4, CH]),
                                op0=OP.min, op1=OP.mult))
                        gdone[g] = mtok
                    if g >= 1:
                        gp = g - 1
                        mslot = mtt2[gp % 2]
                        b.w("tensor", gdone[gp], ytok)
                        def ymm(e, gp=gp, mslot=mslot):
                            for ih in range(2):
                                for hh in range(4):
                                    h = gp * 4 + hh
                                    col = pmm0[:, ih * 256 + hh * HD: ih * 256 + (hh + 1) * HD]
                                    for jh in range(2):
                                        e.matmul(col,
                                                 mslot[:, jh * 1024 + hh * CH + ih * 128:
                                                     jh * 1024 + hh * CH + ih * 128 + 128],
                                                 xw[:, jh * DSSM + h * HD: jh * DSSM + (h + 1) * HD],
                                                 start=(jh == 0), stop=False)
                                    ins = e.matmul(col, di16[:, h * 128:(h + 1) * 128],
                                                   xtt[:, ih * DSSM + h * HD: ih * DSSM + (h + 1) * HD],
                                                   start=False, stop=True)
                            return ins
                        tym = b.prod("tensor", "t", ymm)
                        def yof(e, gp=gp):
                            for ih in range(2):
                                for hh in range(4):
                                    h = gp * 4 + hh
                                    ins = e.matmul(pmm1[:, ih * 256 + hh * HD: ih * 256 + (hh + 1) * HD],
                                                   bfc[:, 256 + ih * 128: 256 + (ih + 1) * 128],
                                                   sused16[:, h * HD:(h + 1) * HD],
                                                   start=True, stop=True)
                            return ins
                        tyo = b.prod("tensor", "t", yof)
                        mrd[gp % 2] = tyo
                        b.w("vector", tym, tyo)
                        for ih in range(2):
                            b.w("vector", b.now("v"))
                            gm1 = b.prod("vector", "v", lambda e, gp=gp, ih=ih: (
                                e.tensor_tensor(out=tmp3[:, 0:256],
                                                in0=pmm1[:, ih * 256:(ih + 1) * 256],
                                                in1=szex[:, ih * DSSM + gp * 256: ih * DSSM + (gp + 1) * 256],
                                                op=OP.mult),
                                e.tensor_tensor(out=tmp3[:, 256:512],
                                                in0=pmm0[:, ih * 256:(ih + 1) * 256],
                                                in1=szt[:, ih * DSSM + gp * 256: ih * DSSM + (gp + 1) * 256],
                                                op=OP.mult))[-1], 2)
                            b.w("vector", gm1)
                            ytok = b.prod("vector", "v", lambda e, gp=gp, ih=ih: e.tensor_tensor(
                                out=yt[:, ih * DSSM + gp * 256: ih * DSSM + (gp + 1) * 256],
                                in0=tmp3[:, 0:256], in1=tmp3[:, 256:512], op=OP.add))
                ytAll = b.now("v")

                for ih in range(2):
                    b.w("scalar", ytAll)
                    ssq = b.prod("scalar", "s", lambda e, ih=ih: e.activation(
                        out=xw[:, ih * DSSM:(ih + 1) * DSSM],
                        in_=yt[:, ih * DSSM:(ih + 1) * DSSM], func=AF.Square,
                        accum_out=rsc3[:, 0:1]))
                    b.w("vector", ssq)
                    vrs = b.prod("vector", "v", lambda e: e.tensor_scalar(
                        out=rsc3[:, 1:2], in0=rsc3[:, 0:1], scalar1=1.0 / DSSM,
                        scalar2=RMS_EPS, op0=OP.mult, op1=OP.add))
                    b.w("scalar", vrs)
                    srt = b.prod("scalar", "s", lambda e: e.activation(
                        out=rsc3[:, 2:3], in_=rsc3[:, 1:2], func=AF.Ln))
                    b.w("scalar", srt)
                    vrc = b.prod("scalar", "s", lambda e: e.activation(
                        out=rsc3[:, 3:4], in_=rsc3[:, 2:3], func=AF.Exp, scale=-0.5))
                    evtok2 = [None, None]
                    b.w("tensor", b.now("s"))
                    for half in range(4):
                        pt = PT[half % 2]
                        b.w("tensor", evtok2[half % 2])
                        def tr(e, ih=ih, half=half, pt=pt):
                            for qq in range(4):
                                kt = half * 4 + qq
                                ins = e.transpose(pt[:, qq * 128:(qq + 1) * 128],
                                                  yt[:, ih * DSSM + kt * 128: ih * DSSM + (kt + 1) * 128],
                                                  c_id16[:])
                            return ins
                        tt = b.prod("tensor", "t", tr)
                        b.w("vector", tt)
                        evtok2[half % 2] = b.prod("vector", "v",
                            lambda e, half=half, pt=pt: e.tensor_copy(
                                out=yf[:, half * 512:(half + 1) * 512], in_=pt[:, 0:512]))
                    b.w("tensor", b.now("v"))
                    b.w("vector", vrc)
                    for nd in range(2):
                        pb = PMM[nd % 2]
                        def mmo(e, nd=nd, pb=pb):
                            for kt in range(16):
                                ins = e.matmul(pb[:, 0:512], yf[:, kt * 128:(kt + 1) * 128],
                                               ar1[:, kt * D + nd * 512: kt * D + (nd + 1) * 512],
                                               start=(kt == 0), stop=(kt == 15))
                            return ins
                        tmo = b.prod("tensor", "t", mmo)
                        b.w("scalar", tmo, vrc)
                        sat = b.prod("scalar", "s", lambda e, nd=nd, pb=pb: e.activation(
                            out=att[:, nd * 512:(nd + 1) * 512], in_=pb[:, 0:512],
                            func=AF.Copy, scale=rsc3[:, 3:4]))
                        b.w("tensor", sat)
                    b.w("vector", b.now("s"))
                    vxb = b.prod("vector", "v", lambda e: e.tensor_tensor(
                        out=att[:, 0:D], in0=att[:, 0:D], in1=boutrep[:], op=OP.add))
                    b.w("vector", vxb)
                    vx2 = b.prod("vector", "v", lambda e, ih=ih: e.tensor_tensor(
                        out=x1c3[:, ih * D:(ih + 1) * D], in0=att[:, 0:D],
                        in1=x1c3[:, ih * D:(ih + 1) * D], op=OP.add))
                    b.w("scalar", vx2)
                b.w("gpsimd", b.now("v"))
                x2_sp = b.dma("gpsimd", lambda e, c=c: [
                    e.dma_start(out=x2_scr[c * CH: c * CH + 128, :], in_=x1c3[:, 0:D]),
                    e.dma_start(out=x2_scr[c * CH + 128:(c + 1) * CH, :], in_=x1c3[:, D:2 * D])], 2)
                prev_t = b.now("t")
                prev_v = b.now("v")
            b.barrier()

        stop = DEBUG["stop"]
        p0()
        if stop != "p0":
            ffn_phase(u, x1_scr, with_halo=True)
        if stop not in ("p0", "p1"):
            p2()
        if stop not in ("p0", "p1", "p2"):
            px()
        if stop not in ("p0", "p1", "p2", "px"):
            p3()
        if stop is None:
            b.w("sync", DT["w1scr"], DT["w2scr"])
            b.dma("sync", lambda e: e.dma_start(out=ar1[:, 0:8 * HF], in_=w1t_scr))
            b.dma("sync", lambda e: e.dma_start(out=arq[:], in_=w2q_scr))
            d4 = b.dma("sync", lambda e: e.dma_start(out=wstg[0:1, 0:D], in_=b2_row))
            b.w("vector", d4)
            v4a = b.prod("vector", "v", lambda e: e.tensor_copy(out=lnt[0:1, 0:D].bitcast(F16)[0:1, 0:D], in_=wstg[0:1, 0:D]))
            b.w("vector", v4a)
            v4b = b.prod("vector", "v", lambda e: e.tensor_copy(out=wstg[0:1, 0:D], in_=lnt[0:1, 0:D].bitcast(F16)[0:1, 0:D]))
            b.w("tensor", v4b)
            t4 = b.prod("tensor", "t", lambda e: (
                e.matmul(paux[:, 0:512], c_ones[:], wstg[0:1, 0:512], start=True, stop=True),
                e.matmul(paux[:, 512:1024], c_ones[:], wstg[0:1, 512:1024], start=True, stop=True))[-1], 2)
            b.w("vector", t4)
            b.prod("vector", "v", lambda e: (
                e.tensor_copy(out=c_b2rep[:, 0:512], in_=paux[:, 0:512]),
                e.tensor_copy(out=c_b2rep[:, 512:1024], in_=paux[:, 512:1024]))[-1], 2)
            b.barrier()
            ffn_phase(x2_scr, out, with_halo=False)
        b.barrier()
        block = E(nc.Block())
        b.finalize(block)
    return nc, inp


def _host_prep(inputs):
    u_full = np.ascontiguousarray(np.asarray(inputs["L0_attn_u"], np.float32).reshape(16384, D))
    W_in = np.ascontiguousarray(np.asarray(inputs["L0_attn_W_in"], np.float32))
    b_in = np.asarray(inputs["L0_attn_b_in"], np.float32)
    conv_w = np.asarray(inputs["L0_attn_conv_w"], np.float32)
    conv_b = np.asarray(inputs["L0_attn_conv_b"], np.float32)
    dt_bias = np.asarray(inputs["L0_attn_dt_bias"], np.float32)
    A_log = np.asarray(inputs["L0_attn_A_log"], np.float32)
    Dv = np.asarray(inputs["L0_attn_D"], np.float32)
    norm_w = np.asarray(inputs["L0_attn_norm_w"], np.float32)
    W_out = np.ascontiguousarray(np.asarray(inputs["L0_attn_W_out"], np.float32))
    b_out = np.asarray(inputs["L0_attn_b_out"], np.float32)
    w1 = np.ascontiguousarray(np.asarray(inputs["L0_ffn_w1"], np.float32))
    b1 = np.asarray(inputs["L0_ffn_b1"], np.float32)
    w2 = np.ascontiguousarray(np.asarray(inputs["L0_ffn_w2"], np.float32))
    b2 = np.asarray(inputs["L0_ffn_b2"], np.float32)

    b1_col = np.ascontiguousarray(b1.reshape(HF // 128, 128).T)
    binx = b_in[DSSM:DPROJ]
    bin_col = np.zeros((128, NXT + 1), np.float32)
    for mf in range(NXT):
        bin_col[:, mf] = binx[mf * 128:(mf + 1) * 128]
    bin_col[0:32, NXT] = binx[NXT * 128: NXT * 128 + 32]
    convw_col = np.zeros((128, NXT * KCONV), np.float32)
    convb_col = np.zeros((128, NXT), np.float32)
    for mf in range(NXT):
        for k in range(KCONV):
            convw_col[:, mf * KCONV + k] = conv_w[mf * 128:(mf + 1) * 128, k]
        convb_col[:, mf] = conv_b[mf * 128:(mf + 1) * 128]
    normw_col = np.ascontiguousarray(norm_w.reshape(DSSM // 128, 128).T)

    trilT = np.zeros((128, 2 * CH), np.float32)
    for jh in range(2):
        for p in range(128):
            trilT[p, jh * CH:(jh + 1) * CH] = ((jh * 128 + p) <= np.arange(CH))
    eexp = np.zeros((128, DSSM), np.float16)
    for h in range(NH):
        eexp[h, h * HD:(h + 1) * HD] = 1.0
    bdones = np.zeros((128, 4 * CH), np.float16)
    for p in range(4):
        bdones[p, p * CH:(p + 1) * CH] = 1.0
        bdones[4 + p, p * CH:(p + 1) * CH] = 1.0
    segones = np.zeros((128, 128), np.float16)
    segones[0:8, :] = 1.0
    repl16 = np.zeros((128, 128), np.float16)
    repl16[0, :] = 1.0
    repl32 = np.zeros((128, 128), np.float32)
    repl32[0, :] = 1.0

    w1t_col = np.ascontiguousarray(
        w1.T.reshape(8, 128, HF).transpose(1, 0, 2).reshape(128, 8 * HF))
    w2t_col = np.ascontiguousarray(
        w2.T.reshape(32, 128, D).transpose(1, 0, 2).reshape(128, 32 * D))
    wint_col = np.ascontiguousarray(
        W_in.T.reshape(8, 128, DPROJ).transpose(1, 0, 2).reshape(128, 8 * DPROJ))
    woutt_col = np.ascontiguousarray(
        W_out.T.reshape(16, 128, D).transpose(1, 0, 2).reshape(128, 16 * D))

    base = dict(
        wint=wint_col, w1t=w1t_col, w2t=w2t_col, woutt=woutt_col,
        b1_col=b1_col, bin_col=bin_col,
        bz_row=np.ascontiguousarray(b_in[0:DSSM].reshape(1, DSSM)).astype(np.float16),
        b2_row=np.ascontiguousarray(b2.reshape(1, D)),
        bout_row=np.ascontiguousarray(b_out.reshape(1, D)).astype(np.float16),
        convw_col=convw_col, convb_col=convb_col,
        dtbias_col=np.ascontiguousarray(dt_bias.reshape(32, 1)),
        alog_col=np.ascontiguousarray(A_log.reshape(32, 1)),
        d_row=np.ascontiguousarray(Dv.reshape(1, NH)), normw_col=normw_col,
        ident16=np.eye(128).astype(np.float16), ident32=np.eye(32).astype(np.float32),
        ones_row=np.ones((1, 128), np.float32),
        ones16=np.ones((16, 128), np.float16),
        trilT=trilT.astype(np.float16), eexp=eexp, bdones=bdones,
        segones=segones, repl16=repl16, repl32=repl32,
    )
    in_maps = []
    for c in range(NCORES):
        m = dict(base)
        m["u"] = u_full[c * T:(c + 1) * T]
        m["u_halo"] = (np.zeros((128, D), np.float32) if c == 0
                       else np.ascontiguousarray(u_full[c * T - 128: c * T]))
        mc = np.zeros((128, NCORES), np.float32)
        mc[:, 0:c] = 1.0
        m["mcols"] = mc
        m["imcols"] = 1.0 - mc
        mn = np.zeros((128, 1), np.float32)
        for p in range(7):
            mn[p, 0] = 1.0 if (p + 1) < c else 0.0
        m["mnext"] = mn
        m["imnext"] = 1.0 - mn
        m["halo_mask"] = np.full((128, 1), 0.0 if c == 0 else 1.0, np.float32)
        in_maps.append(m)
    return in_maps


def kernel(**inputs):
    in_maps = _host_prep(inputs)
    nc, _ = build_nc()
    res = run_bass_kernel_spmd(nc, in_maps, core_ids=list(range(NCORES)))
    parts = [np.asarray(res.results[c]["out"]) for c in range(NCORES)]
    return np.concatenate(parts, axis=0).reshape(1, 16384, D).astype(np.float32)
